# version 01 (final)
# baseline (speedup 1.0000x reference)
import math
import numpy as np
import ml_dtypes
from contextlib import ExitStack

import concourse.bass as bass
import concourse.mybir as mybir
from concourse.bass_utils import run_bass_kernel_spmd

F32 = mybir.dt.float32
BF16 = mybir.dt.bfloat16
I32 = mybir.dt.int32
AF = mybir.ActivationFunctionType
ALU = mybir.AluOpType
AX = mybir.AxisListType

NCORES = 8
T = 2048
D = 1024
NC = D // 128
SEQ = 4096
LN_EPS = 1e-5


STREAMS = ("pe", "dve", "act", "gp", "sp")
_CSEM = {"pe": "c_pe", "dve": "c_dve", "act": "c_act", "gp": "c_gp"}
_DCH = {"sp": [f"sp{i}" for i in range(4)],
        "gp": [f"gp{i}" for i in range(8)],
        "act": [f"ac{i}" for i in range(2)]}
SEM_NAMES = (["c_pe", "c_dve", "c_act", "c_gp"]
             + _DCH["sp"] + _DCH["gp"] + _DCH["act"]
             + [f"cc{i}" for i in range(8)])


class Prog:

    def __init__(self, nc):
        self.nc = nc
        self.items = {s: [] for s in STREAMS}
        self.sems = {}
        self.counts = {n: 0 for n in SEM_NAMES}
        self.last_wait = {s: {} for s in STREAMS}
        self.dma_rr = {"sp": 0, "gp": 0, "act": 0}
        self.cc_idx = 0

    def make_sems(self, es):
        for n in SEM_NAMES:
            self.sems[n] = es.enter_context(self.nc.semaphore(n))

    def _emit(self, stream, fn, waits, sig_sem, sig_amt):
        wl = []
        for ev in waits:
            if ev is None:
                continue
            sn, val = ev
            if val > self.last_wait[stream].get(sn, -1):
                self.last_wait[stream][sn] = val
                wl.append((sn, val))
        ev = None
        sig = None
        if sig_sem is not None:
            self.counts[sig_sem] += sig_amt
            ev = (sig_sem, self.counts[sig_sem])
            sig = (sig_sem, sig_amt)
        self.items[stream].append((wl, fn, sig))
        return ev

    def op(self, stream, fn, waits=(), sig=False):
        return self._emit(stream, fn, waits, _CSEM[stream] if sig else None, 1)

    def mm(self, out, lhsT, rhs, start=True, stop=True, waits=(), sig=False, **kw):
        return self.op("pe", lambda e, o=out, l=lhsT, r=rhs, a=start, b=stop, kw=kw:
                       e.matmul(o, l, r, start=a, stop=b, **kw), waits, sig)

    def transpose(self, out, in_, ident, waits=(), sig=False):
        return self.op("pe", lambda e, o=out, i=in_, d=ident: e.transpose(o, i, d),
                       waits, sig)

    def copy(self, eng, out, in_, waits=(), sig=False):
        return self.op(eng, lambda e, o=out, i=in_: e.tensor_copy(o, i), waits, sig)

    def tt(self, eng, out, in0, in1, alu, waits=(), sig=False):
        return self.op(eng, lambda e, o=out, a=in0, b=in1, u=alu:
                       e.tensor_tensor(o, a, b, u), waits, sig)

    def ts(self, eng, out, in0, s1, s2, op0, op1=None, accum=None, waits=(), sig=False):
        def f(e, o=out, a=in0, x=s1, y=s2, u=op0, w=op1, ac=accum):
            if w is None:
                return e.tensor_scalar(o, a, x, y, u, accum_out=ac)
            return e.tensor_scalar(o, a, x, y, u, w, accum_out=ac)
        return self.op(eng, f, waits, sig)

    def stt(self, eng, out, in0, scalar, in1, op0, op1, accum=None, waits=(), sig=False):
        return self.op(eng, lambda e, o=out, a=in0, s=scalar, b=in1, u=op0, w=op1, ac=accum:
                       e.scalar_tensor_tensor(o, a, s, b, u, w, accum_out=ac), waits, sig)

    def act(self, out, in_, func, bias=0.0, scale=1.0, accum=None, waits=(), sig=False):
        return self.op("act", lambda e, o=out, i=in_, f=func, b=bias, s=scale, ac=accum:
                       e.activation(o, i, f, bias=b, scale=s, accum_out=ac), waits, sig)

    def red(self, eng, out, in_, alu, negate=False, waits=(), sig=False):
        return self.op(eng, lambda e, o=out, i=in_, u=alu, n=negate:
                       e.tensor_reduce(o, i, AX.X, u, negate=n), waits, sig)

    def recip(self, out, in_, waits=(), sig=False):
        return self.op("dve", lambda e, o=out, i=in_: e.reciprocal(o, i), waits, sig)

    def memset(self, eng, ap, val, waits=(), sig=False):
        return self.op(eng, lambda e, a=ap, v=val: e.memset(a, v), waits, sig)

    def dma(self, stream, out, in_, waits=(), sig=True, ch=None, **kw):
        chs = _DCH[stream]
        if ch is None:
            ch = self.dma_rr[stream] % len(chs)
            self.dma_rr[stream] += 1
        sem = chs[ch]
        waits = list(waits) + [(sem, self.counts[sem])] if self.counts[sem] > 0 \
            else list(waits)
        return self._emit(stream, lambda e, o=out, i=in_, kw=kw: e.dma_start(out=o, in_=i, **kw),
                          waits, sem, 16)

    def cc_allgather(self, cin, cout, waits=()):
        rg = [[0, 1], [2, 3], [4, 5], [6, 7]]
        sem = f"cc{self.cc_idx % 8}"
        self.cc_idx += 1
        return self._emit("gp", lambda e, a=cin, b=cout, g=rg:
                          e.collective_compute("AllGather", ALU.bypass,
                                               replica_groups=g, ins=[a.opt()], outs=[b.opt()]),
                          waits, sem, 1)

    def all_events(self):
        return [(n, self.counts[n]) for n in SEM_NAMES if self.counts[n] > 0]

    def finalize_into(self, block):
        prog = self

        def go_factory(stream):
            def go(eng):
                for wl, fn, sig in prog.items[stream]:
                    for (sn, val) in wl:
                        eng.wait_ge(prog.sems[sn], val)
                    if fn is None:
                        continue
                    ins = fn(eng)
                    if sig is not None:
                        ins.then_inc(prog.sems[sig[0]], sig[1])
            return go

        block.sync(go_factory("sp"))
        block.gpsimd(go_factory("gp"))
        block.tensor(go_factory("pe"))
        block.scalar(go_factory("act"))
        block.vector(go_factory("dve"))



class Ctx:
    pass


def alloc_common(nc, es):
    c = Ctx()
    c.nc = nc
    sb = lambda name, shape, dt: es.enter_context(nc.sbuf_tensor(name, shape, dt))
    psa = lambda name, shape, dt: es.enter_context(nc.psum_tensor(name, shape, dt))
    c.XF = sb("XF", [128, NC * T], F32)
    c.XN = sb("XN", [128, NC * T], BF16)
    c.BIG0 = sb("BIG0", [128, 16384], BF16)
    c.BIG1 = sb("BIG1", [128, 16384], BF16)
    c.SCR32 = sb("SCR32", [128, 4096], F32)
    c.WA = sb("WA", [128, 4096], BF16)
    c.WB = sb("WB", [128, 4368], BF16)
    c.SMALL = sb("SMALL", [128, 512], F32)
    c.SQ = sb("SQ", [128, 2048], BF16)
    c.ROWS = sb("ROWS", [1, 4096], BF16)
    c.CONST = sb("CONST", [128, 320], BF16)
    c.CONSTF = sb("CONSTF", [128, 16], F32)
    c.META = sb("META", [128, 8], F32)
    c.PS_A = psa("PS_A", [128, 2048], F32)
    c.PS_B = psa("PS_B", [128, 512], F32)
    c.PS_C = psa("PS_C", [128, 512], F32)
    c.PS_T = psa("PS_T", [128, 1024], BF16)
    c.PS_U = psa("PS_U", [128, 1024], BF16)
    c.rows_guard = []
    c.psc_guard = []
    return c


def emit_consts(c, p, meta_ap):
    c.ev_meta = p.dma("sp", c.META[:, :], meta_ap)
    e0 = p.memset("gp", c.CONST[:, 0:128], 1.0, sig=True)
    p.op("gp", lambda g: g.affine_select(
        c.CONST[:, 0:128], c.CONST[:, 0:128], pattern=[[-1, 128]],
        compare_op=ALU.is_equal, fill=0.0, base=0, channel_multiplier=1),
        waits=[e0])
    p.memset("gp", c.CONST[:, 128:129], 1.0)
    p.memset("gp", c.CONST[:, 129:130], 1.0 / D)
    p.memset("gp", c.CONST[0:1, 130:258], 1.0)
    p.memset("gp", c.CONSTF[:, 0:1], 1.0)
    p.memset("gp", c.CONSTF[:, 2:3], LN_EPS)
    p.memset("gp", c.CONSTF[:, 3:4], 1.0 / 127.0)
    p.memset("gp", c.CONSTF[:, 6:7], -0.5)
    p.memset("gp", c.CONSTF[:, 7:8], -1.0)
    p.memset("gp", c.CONSTF[:, 10:11], 0.08838834764831845)
    c.ev_consts = p.memset("gp", c.CONSTF[:, 1:2], 1.0 / D, sig=True)
    c.id128 = c.CONST[:, 0:128]
    c.ones_bf = c.CONST[:, 128:129]
    c.ones_row = c.CONST[0:1, 130:258]
    c.onesD_bf = c.CONST[:, 129:130]
    c.ones_f = c.CONSTF[:, 0:1]
    c.eps_col = c.CONSTF[:, 2:3]
    c.inv127_col = c.CONSTF[:, 3:4]
    c.neghalf_col = c.CONSTF[:, 6:7]
    c.negone_col = c.CONSTF[:, 7:8]
    c.rsqrt128_col = c.CONSTF[:, 10:11]
    c.onesD_f = c.CONSTF[:, 1:2]
    return c.ev_consts


def emit_recip(c, p, out, in_, waits=()):
    e = p.act(out, in_, AF.Ln, waits=list(waits), sig=True)
    return p.act(out, out, AF.Exp, scale=c.negone_col, waits=[e, c.ev_consts], sig=True)


def emit_rsqrt(c, p, out, in_, bias, waits=()):
    e = p.act(out, in_, AF.Ln, bias=bias, waits=list(waits), sig=True)
    return p.act(out, out, AF.Exp, scale=c.neghalf_col, waits=[e, c.ev_consts], sig=True)


def _chunk3(sb_t, stride, tok0, ntok):
    full = sb_t[:, 0:NC * stride].rearrange("p (cd s) -> p cd s", cd=NC)
    return full[:, :, tok0:tok0 + ntok]


def _bcast3(ps_ap, nmid, ntok):
    return ps_ap[:, 0:ntok].rearrange("p (o s) -> p o s", o=1).broadcast_to(
        [128, nmid, ntok])



def emit_ln_stats(c, p, src_sb, stride, n_tok, psc_off, small_off, dep_evs):
    ntt = n_tok // 128
    dep = list(dep_evs) + [c.ev_consts]
    ev_mm = []
    for tt in range(ntt):
        sqt = c.SQ[:, (tt % 2) * 1024:(tt % 2) * 1024 + 1024]
        sq3 = sqt.rearrange("p (cd s) -> p cd s", cd=NC)
        src3 = _chunk3(src_sb, stride, tt * 128, 128)
        wsq = list(dep) + ([ev_mm[-1]] if tt >= 2 else [])
        ev_sq = p.tt("dve", sq3, src3, src3, ALU.mult, waits=wsq, sig=True)
        for cd in range(NC):
            lhs = src_sb[:, cd * stride + tt * 128: cd * stride + tt * 128 + 128]
            p.mm(c.PS_B[:, psc_off + tt:psc_off + tt + 1], lhs, c.onesD_f,
                 start=(cd == 0), stop=(cd == NC - 1),
                 waits=(dep + c.psc_guard) if (tt == 0 and cd == 0) else ())
        e = None
        for cd in range(NC):
            lhs = sqt[:, cd * 128:(cd + 1) * 128]
            e = p.mm(c.PS_B[:, psc_off + 16 + tt:psc_off + 16 + tt + 1], lhs,
                     c.onesD_bf, start=(cd == 0), stop=(cd == NC - 1),
                     waits=[ev_sq] if cd == 0 else (), sig=(cd == NC - 1))
        ev_mm.append(e)
    mcol = c.SMALL[:, small_off:small_off + ntt]
    rcol = c.SMALL[:, small_off + 16:small_off + 16 + ntt]
    tcol = c.SMALL[:, small_off + 32:small_off + 32 + ntt]
    ev = p.copy("dve", mcol, c.PS_B[:, psc_off:psc_off + ntt], waits=ev_mm, sig=True)
    ev = p.tt("dve", tcol, mcol, mcol, ALU.mult, waits=[ev], sig=True)
    ev = p.tt("dve", tcol, c.PS_B[:, psc_off + 16:psc_off + 16 + ntt], tcol,
              ALU.subtract, waits=[ev], sig=True)
    ev = emit_rsqrt(c, p, tcol, tcol, c.eps_col, waits=[ev])
    c.psc_guard = [ev]
    mr_bf = c.SQ[:, 0:32]
    ev1 = p.copy("dve", mr_bf[:, 0:ntt], mcol, waits=[ev], sig=True)
    ev2 = p.copy("dve", mr_bf[:, 16:16 + ntt], tcol, waits=[ev1], sig=True)
    evt = p.transpose(c.PS_T[0:32, 0:128], mr_bf[:, 0:32], c.id128,
                      waits=[ev2, c.ev_consts], sig=True)
    trs = c.SQ[0:32, 1024:1152]
    ev3 = p.copy("dve", trs, c.PS_T[0:32, 0:128], waits=[evt], sig=True)
    evd1 = p.dma("gp", c.ROWS[0:1, 0:n_tok].rearrange("a (tt x) -> a tt x", tt=ntt),
                 trs[0:ntt, :], waits=[ev3] + c.rows_guard)
    evd2 = p.dma("gp", c.ROWS[0:1, 2048:2048 + n_tok].rearrange("a (tt x) -> a tt x", tt=ntt),
                 trs[16:16 + ntt, :], waits=[ev3])
    return [evd1, evd2]


def emit_ln_full(c, p, dep_evs):
    evd = emit_ln_stats(c, p, c.XF, T, T, 0, 0, dep_evs)
    ev_out = None
    for h in range(2):
        mb = c.PS_A[:, 0:1024]
        rb = c.PS_A[:, 1024:2048]
        evb = []
        for j in range(0, 1024, 512):
            w0 = evd + [c.ev_consts] + ([ev_out] if ev_out else [])
            evb.append(p.mm(mb[:, j:j + 512], c.ones_row,
                            c.ROWS[0:1, h * 1024 + j:h * 1024 + j + 512],
                            waits=w0, sig=True))
            evb.append(p.mm(rb[:, j:j + 512], c.ones_row,
                            c.ROWS[0:1, 2048 + h * 1024 + j:2048 + h * 1024 + j + 512],
                            sig=True))
        src3 = _chunk3(c.XF, T, h * 1024, 1024)
        dst3 = _chunk3(c.XN, T, h * 1024, 1024)
        tmp = c.SCR32[:, 0:NC * 512]
        for s2 in range(2):
            sl = slice(s2 * 512, s2 * 512 + 512)
            tmp3 = c.SCR32[:, 0:NC * 512].rearrange("p (cd s) -> p cd s", cd=NC)
            e1 = p.tt("dve", tmp3, src3[:, :, sl], _bcast3(mb[:, sl], NC, 512),
                      ALU.subtract, waits=evb + list(dep_evs), sig=True)
            ev_out = p.tt("dve", dst3[:, :, sl], tmp3, _bcast3(rb[:, sl], NC, 512),
                          ALU.mult, waits=[e1], sig=True)
    c.rows_guard = [ev_out]
    return ev_out



def emit_l0ffn(c, p, w, dep_evs):
    F = 4096
    ev_ln = emit_ln_full(c, p, dep_evs)
    b1c = c.SMALL[:, 64:96]
    b2c = c.SMALL[:, 96:104]
    gam = c.SMALL[:, 104:112]
    bet = c.SMALL[:, 112:120]
    e_b1 = p.dma("sp", b1c, w["b1c"])
    e_b2 = p.dma("sp", b2c, w["b2c"])
    e_g = p.dma("sp", gam, w["gamc"])
    e_be = p.dma("sp", bet, w["betc"])
    ev_g05 = p.ts("dve", gam, gam, 0.5, None, ALU.mult, waits=[e_g], sig=True)
    ev_b05 = p.ts("dve", bet, bet, 0.5, None, ALU.mult, waits=[e_be], sig=True)

    TT = 512
    ntt = T // TT
    w1d = w["w1T"].rearrange("(cd pp) f -> pp cd f", pp=128)
    w2d = w["w2"].rearrange("(fc pp) dd -> pp fc dd", pp=128)
    ev_res_last = None
    ev_down_last = None
    ev_w_hist = []
    ev_evac_hist = []
    ev_dt_hist = []
    for tt in range(ntt):
        ev_hmm = None
        for fi in range(F // 128):
            wbuf = (c.WA if fi % 2 == 0 else c.WB)[:, 0:1024]
            w3 = wbuf.rearrange("p (cd f) -> p cd f", cd=NC)
            if fi < 2:
                wwait = [ev_down_last] if ev_down_last is not None else []
            else:
                wwait = [ev_w_hist[-2][1]]
            ev_w = p.dma("gp", w3, w1d[:, :, fi * 128:(fi + 1) * 128], waits=wwait)
            hps = (c.PS_B if fi % 2 == 0 else c.PS_C)[:, 0:512]
            ev_hmm = None
            for cd in range(NC):
                wv = []
                if cd == 0:
                    wv = [ev_w, ev_ln]
                    if len(ev_evac_hist) >= 2:
                        wv.append(ev_evac_hist[-2])
                    if fi == 0 and ev_res_last is not None:
                        wv.append(ev_res_last)
                ev_hmm = p.mm(hps, wbuf[:, cd * 128:(cd + 1) * 128],
                              _xn_tile(c, cd, tt * TT, TT),
                              start=(cd == 0), stop=(cd == NC - 1), waits=wv,
                              sig=(cd == NC - 1))
            ev_w_hist.append((ev_w, ev_hmm))
            hsb = c.BIG0[:, fi * TT:(fi + 1) * TT]
            wv = [ev_hmm, e_b1]
            if ev_down_last is not None:
                wv.append(ev_down_last)
            e = p.act(hsb, hps, AF.Relu, bias=b1c[:, fi:fi + 1], waits=wv, sig=True)
            ev_evac_hist.append(e)
        for dt_ in range(NC):
            wbuf = (c.WA if dt_ % 2 == 0 else c.WB)[:, 0:4096]
            w3 = wbuf.rearrange("p (fc f) -> p fc f", fc=32)
            wwait = [ev_w_hist[-1][1]]
            if dt_ >= 2:
                wwait = [ev_down_last]
            ev_w = p.dma("gp", w3, w2d[:, :, dt_ * 128:(dt_ + 1) * 128], waits=wwait)
            ops = c.PS_A[:, (dt_ % 4) * 512:(dt_ % 4) * 512 + 512]
            ev_dmm = None
            for fc in range(32):
                wv = []
                if fc == 0:
                    wv = [ev_w, ev_evac_hist[-1]]
                    if len(ev_dt_hist) >= 4:
                        wv.append(ev_dt_hist[-4])
                ev_dmm = p.mm(ops, wbuf[:, fc * 128:(fc + 1) * 128],
                              c.BIG0[:, fc * TT:(fc + 1) * TT],
                              start=(fc == 0), stop=(fc == 31), waits=wv,
                              sig=(fc == 31))
            ev_down_last = ev_dmm
            x2 = c.SCR32[:, dt_ * TT:(dt_ + 1) * TT]
            e = p.stt("dve", x2, ops, b2c[:, dt_:dt_ + 1],
                      _xn_tile(c, dt_, tt * TT, TT), ALU.add, ALU.add,
                      waits=[ev_dmm, e_b2], sig=True)
            ev_dt_hist.append(e)
        evd = emit_ln_stats(c, p, c.SCR32, TT, TT, 256, 128,
                            ev_dt_hist[-NC:])
        mb = c.PS_B[:, 0:512]
        rb = c.PS_C[:, 0:512]
        eb1 = p.mm(mb, c.ones_row, c.ROWS[0:1, 0:512],
                   waits=evd + [ev_evac_hist[-1], c.ev_consts], sig=True)
        eb2 = p.mm(rb, c.ones_row, c.ROWS[0:1, 2048:2048 + 512], sig=True)
        c.rows_guard = [eb2]
        ev_res = None
        for cd in range(NC):
            x2 = c.SCR32[:, cd * TT:(cd + 1) * TT]
            e1 = p.tt("dve", x2, x2, mb, ALU.subtract, waits=[eb1, eb2], sig=False)
            e2 = p.tt("dve", x2, x2, rb, ALU.mult)
            e3 = p.ts("dve", x2, x2, gam[:, cd:cd + 1], bet[:, cd:cd + 1],
                      ALU.mult, ALU.add, waits=[ev_g05, ev_b05])
            xf = c.XF[:, cd * T + tt * TT: cd * T + tt * TT + TT]
            ev_res = p.tt("dve", xf, xf, x2, ALU.add, sig=(cd == NC - 1))
        ev_res_last = ev_res
        c.psc_guard = c.psc_guard + [ev_res]
    return ev_res_last




def emit_l2ffn(c, p, w, dep_evs):
    H3 = 3072
    ev_ln = emit_ln_full(c, p, dep_evs)
    TT = 512
    ntt = T // TT
    wgd = w["wgT"].rearrange("(cd pp) f -> pp cd f", pp=128)
    wud = w["wuT"].rearrange("(cd pp) f -> pp cd f", pp=128)
    wdd = w["wdT"].rearrange("(hc pp) dd -> pp hc dd", pp=128)
    ev_res_last = None
    ev_down_last = None
    ev_w_hist = []
    ev_evac_hist = []
    ev_stt_hist = []
    for tt in range(ntt):
        for hc in range(H3 // 128):
            wbuf = (c.WA if hc % 2 == 0 else c.WB)
            if hc < 2:
                wwait = [ev_down_last] if ev_down_last is not None else []
            else:
                wwait = [ev_w_hist[-2][1]]
            ev_wg = p.dma("gp", wbuf[:, 0:1024].rearrange("p (cd f) -> p cd f", cd=NC),
                          wgd[:, :, hc * 128:(hc + 1) * 128], waits=wwait)
            ev_wu = p.dma("gp", wbuf[:, 1024:2048].rearrange("p (cd f) -> p cd f", cd=NC),
                          wud[:, :, hc * 128:(hc + 1) * 128], waits=[])
            gps = c.PS_B[:, 0:512]
            ups = c.PS_C[:, 0:512]
            ev_g = ev_u = None
            for cd in range(NC):
                wv = []
                if cd == 0:
                    wv = [ev_wg, ev_wu, ev_ln]
                    if len(ev_evac_hist) >= 1:
                        wv.append(ev_evac_hist[-1])
                    if hc == 0 and ev_res_last is not None:
                        wv.append(ev_res_last)
                ev_g = p.mm(gps, wbuf[:, cd * 128:(cd + 1) * 128],
                            _xn_tile(c, cd, tt * TT, TT),
                            start=(cd == 0), stop=(cd == NC - 1), waits=wv,
                            sig=(cd == NC - 1))
            for cd in range(NC):
                ev_u = p.mm(ups, wbuf[:, 1024 + cd * 128:1024 + (cd + 1) * 128],
                            _xn_tile(c, cd, tt * TT, TT),
                            start=(cd == 0), stop=(cd == NC - 1),
                            sig=(cd == NC - 1))
            ev_w_hist.append((ev_wg, ev_u))
            gs = c.SQ[:, (hc % 2) * 512:(hc % 2) * 512 + 512]
            e1 = p.act(gs, gps, AF.Silu, scale=c.inv127_col,
                       waits=[ev_g] + ([ev_evac_hist[-2]] if len(ev_evac_hist) >= 2 else []),
                       sig=True)
            e2 = p.tt("dve", c.BIG0[:, hc * TT:(hc + 1) * TT], gs, ups, ALU.mult,
                      waits=[e1, ev_u] + ([ev_down_last] if ev_down_last is not None else []),
                      sig=True)
            ev_evac_hist.append(e2)
        for dt_ in range(NC):
            wbuf2 = (c.WA if dt_ % 2 == 0 else c.WB)[:, 0:3072]
            w3 = wbuf2.rearrange("p (hc f) -> p hc f", hc=24)
            wwait = [ev_w_hist[-1][1]]
            if dt_ >= 2:
                wwait = [ev_down_last]
            ev_w = p.dma("gp", w3, wdd[:, :, dt_ * 128:(dt_ + 1) * 128], waits=wwait)
            ops = c.PS_A[:, (dt_ % 4) * 512:(dt_ % 4) * 512 + 512]
            ev_dmm = None
            for hc in range(24):
                wv = []
                if hc == 0:
                    wv = [ev_w, ev_evac_hist[-1]]
                    if len(ev_stt_hist) >= 4:
                        wv.append(ev_stt_hist[-4])
                ev_dmm = p.mm(ops, wbuf2[:, hc * 128:(hc + 1) * 128],
                              c.BIG0[:, hc * TT:(hc + 1) * TT],
                              start=(hc == 0), stop=(hc == 23), waits=wv,
                              sig=(hc == 23))
            ev_down_last = ev_dmm
            xf = c.XF[:, dt_ * T + tt * TT: dt_ * T + tt * TT + TT]
            e = p.stt("dve", xf, ops, 0.5 / (127.0 * 127.0), xf, ALU.mult, ALU.add,
                      waits=[ev_dmm], sig=True)
            ev_stt_hist.append(e)
            ev_res_last = e
    return ev_res_last




def emit_l3ffn(c, p, w, dep_evs):
    F = 4096
    ev_ln = emit_ln_full(c, p, dep_evs)
    e_rw = p.dma("sp", c.SQ[:, 0:8].rearrange("p e -> p () e", ), w["router_w"], sig=True) \
        if False else None
    rw_sb = c.SQ[:, 0:64]
    e_rw = p.dma("gp", rw_sb.rearrange("p (cd e) -> p cd e", cd=NC),
                 w["router_w"].rearrange("(cd pp) e -> pp cd e", pp=128),
                 waits=[ev_ln])
    e_rb = p.dma("sp", c.SMALL[:, 176:184], w["rb_rep"])
    ru = c.SMALL[:, 192:320]
    ssum = c.SMALL[:, 320:336]
    mcol = c.SMALL[:, 336:352]
    u2 = c.SMALL[:, 352:368]
    sinv = c.SMALL[:, 496:512]
    scr = c.SMALL[:, 368:496]
    ev_lg = []
    for tt in range(16):
        ps = c.PS_C[:, (tt % 2) * 128:(tt % 2) * 128 + 8]
        ev = None
        for cd in range(NC):
            wv = []
            if cd == 0:
                wv = [e_rw, ev_ln]
                if ev_lg:
                    wv.append(ev_lg[-1])
            ev = p.mm(ps, _xn_tile(c, cd, tt * 128, 128), rw_sb[:, cd * 8:(cd + 1) * 8],
                      start=(cd == 0), stop=(cd == NC - 1), waits=wv, sig=(cd == NC - 1))
        e2 = p.tt("dve", ru[:, tt * 8:(tt + 1) * 8], ps, c.SMALL[:, 176:184],
                  ALU.add, waits=[ev, e_rb], sig=True)
        ev_lg.append(e2)
    ru3 = ru.rearrange("p (tt e) -> p tt e", tt=16)
    p.red("dve", mcol, ru3, ALU.max, waits=[ev_lg[-1]])
    mb3 = mcol.rearrange("p (tt o) -> p tt o", o=1).broadcast_to([128, 16, 8])
    ev_sub = p.tt("dve", ru3, ru3, mb3, ALU.subtract, sig=True)
    ev = p.op("act", lambda e, o=ru, i=ru: e.activation(o, i, AF.Exp),
              waits=[ev_sub], sig=True)
    if c.dbg is not None:
        p.dma("sp", c.dbg[:, 128:256], ru, waits=[ev])
    ev_ss = p.red("dve", ssum, ru3, ALU.add, waits=[ev], sig=True)
    p.red("dve", mcol, ru3, ALU.max)
    m1b = mcol.rearrange("p (tt o) -> p tt o", o=1).broadcast_to([128, 16, 8])
    scr3 = scr.rearrange("p (tt e) -> p tt e", tt=16)
    p.tt("dve", scr3, ru3, m1b, ALU.is_equal)
    p.stt("dve", scr3, scr3, -1e30, ru3, ALU.mult, ALU.add)
    p.red("dve", u2, scr3, ALU.max)
    u2b = u2.rearrange("p (tt o) -> p tt o", o=1).broadcast_to([128, 16, 8])
    p.tt("dve", scr3, ru3, u2b, ALU.is_ge)
    ev_ind = p.tt("dve", scr3, scr3, scr3, ALU.mult, sig=True)
    if c.dbg is not None:
        p.dma("sp", c.dbg[:, 256:384], scr, waits=[ev_ind])
        p.dma("sp", c.dbg[:, 384:400], ssum, waits=[ev_ind])
        p.dma("sp", c.dbg[:, 400:416], u2, waits=[ev_ind])
    p.tt("dve", ru3, ru3, scr3, ALU.mult)
    ev_ri = emit_recip(c, p, sinv, ssum, waits=[ev_ss])
    if c.dbg is not None:
        p.dma("sp", c.dbg[:, 416:432], sinv, waits=[ev_ri])
        p.dma("sp", c.dbg[:, 432:448], ssum, waits=[ev_ri])
    sb3 = sinv.rearrange("p (tt o) -> p tt o", o=1).broadcast_to([128, 16, 8])
    ev_cb = p.tt("dve", ru3, ru3, sb3, ALU.mult, waits=[ev_ri], sig=True)
    comb_bf = c.SQ[:, 128:256]
    ev_cb = p.copy("dve", comb_bf, ru, waits=[ev_cb], sig=True)
    if c.dbg is not None:
        p.dma("sp", c.dbg[:, 0:128], ru, waits=[ev_cb])

    TT = 512
    ntt = T // TT
    upd = w["upT"].rearrange("e (cd pp) f -> pp e cd f", pp=128)
    dnd = w["downT"].rearrange("e (fc pp) dd -> pp e fc dd", pp=128)
    ev_res_last = None
    ev_down_last = None
    ev_w_hist = []
    ev_evac_hist = []
    ev_acc_hist = []
    for tt in range(ntt):
        for ex in range(8):
            cps = c.PS_C[:, 0:512]
            evb = None
            for j in range(4):
                col = comb_bf[:, (tt * 4 + j) * 8 + ex:(tt * 4 + j) * 8 + ex + 1]
                wv = [ev_cb, c.ev_consts]
                if ev_acc_hist:
                    wv.append(ev_acc_hist[-1])
                evb = p.mm(cps[:, j * 128:(j + 1) * 128],
                           col.broadcast_to([128, 128]), c.id128,
                           waits=wv, sig=(j == 3))
            cb_sb = c.SQ[:, 256:768]
            evb = p.copy("dve", cb_sb, cps, waits=[evb], sig=True)
            for fi in range(F // 128):
                wbuf = (c.WA if fi % 2 == 0 else c.WB)[:, 0:1024]
                w3 = wbuf.rearrange("p (cd f) -> p cd f", cd=NC)
                if fi < 2:
                    wwait = [ev_down_last] if ev_down_last is not None else []
                else:
                    wwait = [ev_w_hist[-2][1]]
                ev_w = p.dma("gp", w3, upd[:, ex, :, fi * 128:(fi + 1) * 128],
                             waits=wwait)
                hps = (c.PS_B if fi % 2 == 0 else c.PS_C)[:, 0:512]
                ev_hmm = None
                for cd in range(NC):
                    wv = []
                    if cd == 0:
                        wv = [ev_w, ev_ln, evb]
                        if len(ev_evac_hist) >= 2:
                            wv.append(ev_evac_hist[-2])
                    ev_hmm = p.mm(hps, wbuf[:, cd * 128:(cd + 1) * 128],
                                  _xn_tile(c, cd, tt * TT, TT),
                                  start=(cd == 0), stop=(cd == NC - 1), waits=wv,
                                  sig=(cd == NC - 1))
                ev_w_hist.append((ev_w, ev_hmm))
                big = c.BIG0 if fi < 16 else c.BIG1
                hsb = big[:, (fi % 16) * TT:(fi % 16) * TT + TT]
                wv = [ev_hmm]
                if ev_acc_hist:
                    wv.append(ev_acc_hist[-1])
                e = p.act(hsb, hps, AF.Gelu, waits=wv, sig=True)
                ev_evac_hist.append(e)
            ev_sc = None
            for half in range(2):
                big = c.BIG0 if half == 0 else c.BIG1
                h3 = big[:, 0:16 * TT].rearrange("p (fc s) -> p fc s", fc=16)
                cb3 = cb_sb.rearrange("p (o s) -> p o s", o=1).broadcast_to([128, 16, TT])
                ev_sc = p.tt("dve", h3, h3, cb3, ALU.mult,
                             waits=[ev_evac_hist[-1], evb], sig=True)
            for dt_ in range(NC):
                wbuf2 = (c.WA if dt_ % 2 == 0 else c.WB)[:, 0:4096]
                w3 = wbuf2.rearrange("p (fc f) -> p fc f", fc=32)
                wwait = [ev_w_hist[-1][1]]
                if dt_ >= 2:
                    wwait = [ev_down_last]
                ev_w = p.dma("gp", w3, dnd[:, ex, :, dt_ * 128:(dt_ + 1) * 128],
                             waits=wwait)
                ops = c.PS_A[:, (dt_ % 4) * 512:(dt_ % 4) * 512 + 512]
                ev_dmm = None
                for fc in range(32):
                    wv = []
                    if fc == 0:
                        wv = [ev_w, ev_sc]
                        if len(ev_acc_hist) >= 4:
                            wv.append(ev_acc_hist[-4])
                    big = c.BIG0 if fc < 16 else c.BIG1
                    ev_dmm = p.mm(ops, wbuf2[:, fc * 128:(fc + 1) * 128],
                                  big[:, (fc % 16) * TT:(fc % 16) * TT + TT],
                                  start=(fc == 0), stop=(fc == 31), waits=wv,
                                  sig=(fc == 31))
                ev_down_last = ev_dmm
                acc = c.SCR32[:, dt_ * TT:(dt_ + 1) * TT]
                if ex == 0:
                    e = p.copy("dve", acc, ops, waits=[ev_dmm], sig=True)
                else:
                    e = p.tt("dve", acc, acc, ops, ALU.add, waits=[ev_dmm], sig=True)
                ev_acc_hist.append(e)
        xf3 = _chunk3(c.XF, T, tt * TT, TT)
        a3 = c.SCR32[:, 0:NC * TT].rearrange("p (cd s) -> p cd s", cd=NC)
        ev_res_last = p.stt("dve", xf3, a3, 0.5, xf3, ALU.mult, ALU.add,
                            waits=[ev_acc_hist[-1]], sig=True)
        ev_acc_hist.append(ev_res_last)
    return ev_res_last




def emit_l3attn(c, p, w, dep_evs):
    nc = c.nc
    ev_ln = emit_ln_full(c, p, dep_evs)
    kin = nc.dram_tensor("l3a_kin", [1024, 2048], BF16, kind="Internal").ap()
    kout = nc.dram_tensor("l3a_kout", [2048, 2048], BF16, kind="Internal").ap()
    vin = nc.dram_tensor("l3a_vin", [2048, 1024], BF16, kind="Internal").ap()
    vout = nc.dram_tensor("l3a_vout", [4096, 1024], BF16, kind="Internal").ap()

    bq = c.SMALL[:, 64:88]
    bp = c.SMALL[:, 88:96]
    e_bq = p.dma("sp", bq, w["bqkvc"])
    e_bp = p.dma("sp", bp, w["bprojc"])
    bq8 = c.SMALL[:, 56:64]
    e_bq8 = p.ts("dve", bq8, bq[:, 0:8], 0.125, None, ALU.mult, waits=[e_bq], sig=True)
    p.memset("gp", c.CONSTF[:, 4:5], 0.125)
    ev_sc8 = p.memset("gp", c.CONSTF[:, 5:6], 1.0, sig=True)
    eighth = c.CONSTF[:, 4:5]

    wq = w["Wqkv"].rearrange("(cd pp) o -> pp cd o", pp=128)
    TT = 512
    ev_kdma = []
    ev_vdma = []
    ev_qev = []
    ev_grp = {}
    ev_stage_hist = []
    for oc in range(24):
        wbuf = (c.WA if oc % 2 == 0 else c.WB)[:, 0:1024]
        wwait = [ev_grp[oc - 2]] if oc >= 2 else []
        ev_w = p.dma("gp", wbuf.rearrange("p (cd f) -> p cd f", cd=NC),
                     wq[:, :, oc * 128:(oc + 1) * 128], waits=wwait)
        for tt in range(4):
            ops = (c.PS_B if (oc * 4 + tt) % 2 == 0 else c.PS_C)[:, 0:512]
            ev_mm = None
            for cd in range(NC):
                wv = []
                if cd == 0:
                    wv = [ev_w, ev_ln]
                    if len(ev_stage_hist) >= 2:
                        wv.append(ev_stage_hist[-2])
                ev_mm = p.mm(ops, wbuf[:, cd * 128:(cd + 1) * 128],
                             _xn_tile(c, cd, tt * TT, TT),
                             start=(cd == 0), stop=(cd == NC - 1), waits=wv,
                             sig=(cd == NC - 1))
            ev_grp[oc] = ev_mm
            if oc < 8:
                dst = c.BIG0[:, oc * T + tt * TT: oc * T + tt * TT + TT]
                e = p.act(dst, ops, AF.Identity, bias=bq8[:, oc:oc + 1], scale=eighth,
                          waits=[ev_mm, e_bq8, ev_sc8], sig=True)
                ev_qev.append(e)
                ev_stage_hist.append(e)
            elif oc < 16:
                st = c.BIG1[:, (tt % 4) * 512:(tt % 4) * 512 + 512]
                e = p.act(st, ops, AF.Identity, bias=bq[:, oc:oc + 1],
                          waits=[ev_mm, e_bq] + ([ev_kdma[-4]] if len(ev_kdma) >= 4 else []),
                          sig=True)
                ev_stage_hist.append(e)
                ed = p.dma("sp", kin[(oc - 8) * 128:(oc - 7) * 128,
                                     tt * TT:(tt + 1) * TT], st, waits=[e])
                ev_kdma.append(ed)
            else:
                cv = oc - 16
                st = c.BIG1[:, 2048 + (tt % 2) * 512: 2048 + (tt % 2) * 512 + 512]
                e = p.act(st, ops, AF.Identity, bias=bq[:, oc:oc + 1],
                          waits=[ev_mm, e_bq] + ([ev_vdma[-2]] if len(ev_vdma) >= 2 else []),
                          sig=True)
                ev_stage_hist.append(e)
                stt_ = c.BIG1[:, 4096 + (tt % 2) * 512: 4096 + (tt % 2) * 512 + 512]
                evs = []
                for j in range(4):
                    pst = (c.PS_T if j % 2 == 0 else c.PS_U)[:, 0:128]
                    evt = p.transpose(pst, st[:, j * 128:(j + 1) * 128], c.id128,
                                      waits=[e] + ([evs[-2]] if len(evs) >= 2 else []),
                                      sig=True)
                    ee = p.copy("dve", stt_[:, j * 128:(j + 1) * 128], pst,
                                waits=[evt], sig=True)
                    evs.append(ee)
                vout_ap = vin[tt * TT:(tt + 1) * TT, cv * 128:(cv + 1) * 128]
                vap = vout_ap.rearrange("(j p) d -> p j d", p=128)
                ed = p.dma("sp", vap,
                           stt_.rearrange("p (j d) -> p j d", j=4), waits=[evs[-1]])
                ev_vdma.append(ed)
    ev_agk = p.cc_allgather(kin, kout, waits=ev_kdma)
    ev_agv = p.cc_allgather(vin, vout, waits=ev_vdma)

    nmx_off = 96
    ev_ctx_ev = []
    ev_score_last = {}
    ev_kload = {}
    ev_vload = {}
    ev_exp_hist = []
    ev_red_hist = []
    ev_tr_hist = []
    ev_avmm_hist = []
    for h in range(16):
        par = h % 2
        base = par * 64
        kb = c.WA[base:base + 64, 0:4096]
        vb = c.WB[:, par * 2080: par * 2080 + 2080]
        wv = [ev_agk] + ([ev_score_last[h - 2]] if h - 2 in ev_score_last else [])
        for r in range(2):
            ev_kload[(h, r)] = p.dma("gp", kb[:, r * 2048:(r + 1) * 2048],
                                     kout[r * 1024 + h * 64: r * 1024 + (h + 1) * 64, :],
                                     waits=wv)
            src = vout[r * 2048:(r + 1) * 2048, h * 64:(h + 1) * 64]
            vb3 = vb[:, r * 1040: r * 1040 + 1040].rearrange("p (j d) -> p j d", d=65)
            ev = p.dma("gp", vb3[:, :, 0:64],
                       src.rearrange("(j p) d -> p j d", p=128),
                       waits=[ev_agv] + ([ev_score_last[h - 2]] if h - 2 in ev_score_last else []))
            ev_vload[(h, r)] = p.memset("gp", vb3[:, :, 64:65], 1.0,
                                        waits=[ev], sig=True)
        for qt in range(16):
            qsl = c.BIG0[base:base + 64, (h // 2) * T + qt * 128: (h // 2) * T + qt * 128 + 128]
            for ch in range(4):
                psa_ = c.PS_A[:, (ch % 2) * 1024:(ch % 2) * 1024 + 1024]
                ev_mm = None
                for j in range(2):
                    wv = []
                    if j == 0:
                        wv = [ev_kload[(h, ch // 2)], ev_qev[-1] if ev_qev else None]
                        if len(ev_red_hist) >= 2:
                            wv.append(ev_red_hist[-2])
                        if ch < 2 and len(ev_exp_hist) >= 1:
                            wv.append(ev_exp_hist[-1])
                    ev_mm = p.mm(psa_[:, j * 512:(j + 1) * 512], qsl,
                                 kb[:, (ch // 2) * 2048 + (ch % 2) * 1024 + j * 512:
                                    (ch // 2) * 2048 + (ch % 2) * 1024 + j * 512 + 512],
                                 start=True, stop=True, waits=wv, sig=(j == 1))
                e = p.red("dve", c.SMALL[:, nmx_off + ch:nmx_off + ch + 1], psa_,
                          ALU.max, negate=True, waits=[ev_mm], sig=True)
                ev_red_hist.append(e)
            nm = c.SMALL[:, nmx_off + 4:nmx_off + 5]
            p.tt("dve", nm, c.SMALL[:, nmx_off:nmx_off + 1],
                 c.SMALL[:, nmx_off + 1:nmx_off + 2], ALU.min)
            p.tt("dve", nm, nm, c.SMALL[:, nmx_off + 2:nmx_off + 3], ALU.min)
            ev_nm = p.tt("dve", nm, nm, c.SMALL[:, nmx_off + 3:nmx_off + 4],
                         ALU.min, sig=True)
            bank = c.PS_B if (h * 16 + qt) % 2 == 0 else c.PS_C
            ctxps = bank[:, 0:65]
            navm = 0
            for ch in range(4):
                psa_ = c.PS_A[:, (ch % 2) * 1024:(ch % 2) * 1024 + 1024]
                ev_mm = None
                for j in range(2):
                    wv = []
                    if j == 0:
                        wv = [ev_nm]
                        if len(ev_exp_hist) >= 2:
                            wv.append(ev_exp_hist[-2])
                        if ch == 0:
                            wv.append(ev_red_hist[-1])
                    ev_mm = p.mm(psa_[:, j * 512:(j + 1) * 512], qsl,
                                 kb[:, (ch // 2) * 2048 + (ch % 2) * 1024 + j * 512:
                                    (ch // 2) * 2048 + (ch % 2) * 1024 + j * 512 + 512],
                                 start=True, stop=True, waits=wv, sig=(j == 1))
                pb = c.BIG1[:, ch * 1024:(ch + 1) * 1024]
                ev_exp = p.act(pb, psa_, AF.Exp, bias=nm,
                               waits=[ev_mm, ev_nm] + ([ev_tr_hist[-8]] if len(ev_tr_hist) >= 8 else []),
                               sig=True)
                ev_exp_hist.append(ev_exp)
                for j in range(8):
                    pst = (c.PS_T if j % 2 == 0 else c.PS_U)[:, 128:256]
                    wv = [ev_exp]
                    if len(ev_tr_hist) >= 2:
                        wv.append(ev_tr_hist[-2])
                    if ch == 0 and j == 0 and ev_ctx_ev:
                        wv.append(ev_ctx_ev[-1])
                    evt = p.transpose(pst, pb[:, j * 128:(j + 1) * 128], c.id128,
                                      waits=wv, sig=True)
                    pts = c.BIG1[:, 4096 + ((j % 2)) * 128: 4096 + (j % 2) * 128 + 128]
                    ee = p.copy("dve", pts, pst, waits=[evt] +
                                ([ev_avmm_hist[-2]] if len(ev_avmm_hist) >= 2 else []),
                                sig=True)
                    ev_tr_hist.append(ee)
                    r = ch // 2
                    j2 = (ch % 2) * 8 + j
                    wv2 = [ee, ev_vload[(h, r)]]
                    if navm == 0 and len(ev_ctx_ev) >= 2:
                        wv2.append(ev_ctx_ev[-2])
                    ev_av = p.mm(ctxps, pts,
                                 vb[:, r * 1040 + j2 * 65: r * 1040 + j2 * 65 + 65],
                                 start=(navm == 0), stop=(navm == 31), waits=wv2,
                                 sig=True)
                    ev_avmm_hist.append(ev_av)
                    navm += 1
            lsum = c.SMALL[:, nmx_off + 9:nmx_off + 10]
            ev_ls = p.copy("dve", lsum, ctxps[:, 64:65], waits=[ev_avmm_hist[-1]], sig=True)
            linv = c.SMALL[:, nmx_off + 10:nmx_off + 11]
            ev_li = emit_recip(c, p, linv, lsum, waits=[ev_ls])
            ctx_bf = c.SQ[:, 0:64]
            ev_ce = p.ts("dve", ctx_bf, ctxps[:, 0:64], linv[:, 0:1], None, ALU.mult,
                         waits=[ev_li, ev_avmm_hist[-1]], sig=True)
            evt = p.transpose(c.PS_T[0:64, 256:384], ctx_bf, c.id128,
                              waits=[ev_ce], sig=True)
            dst = c.XN[base:base + 64, (h // 2) * T + qt * 128:
                       (h // 2) * T + qt * 128 + 128]
            ev_cf = p.copy("dve", dst, c.PS_T[0:64, 256:384], waits=[evt], sig=True)
            ev_ctx_ev.append(ev_cf)
        ev_score_last[h] = ev_avmm_hist[-1]
    if c.dbg2 is not None:
        p.dma("gp", c.dbg2[:, :], c.XN[:, :], waits=[ev_ctx_ev[-1]])
    wp = w["Wproj"].rearrange("(cd pp) o -> pp cd o", pp=128)
    ev_res = None
    ev_ogrp = []
    for oc in range(NC):
        wbuf = (c.WA if oc % 2 == 0 else c.WB)[:, 0:1024]
        wwait = [ev_ctx_ev[-1]] if oc < 2 else [ev_ogrp[(oc - 2) * 4 + 3]]
        if oc < 2:
            wwait.append(ev_score_last[15])
        ev_w = p.dma("gp", wbuf.rearrange("p (cd f) -> p cd f", cd=NC),
                     wp[:, :, oc * 128:(oc + 1) * 128], waits=wwait)
        for tt in range(4):
            ops = (c.PS_B if (oc * 4 + tt) % 2 == 0 else c.PS_C)[:, 0:512]
            ev_mm = None
            for cd in range(NC):
                wv = []
                if cd == 0:
                    wv = [ev_w, ev_ctx_ev[-1]]
                    if ev_res is not None:
                        wv.append(ev_res)
                ev_mm = p.mm(ops, wbuf[:, cd * 128:(cd + 1) * 128],
                             c.XN[:, cd * T + tt * TT: cd * T + tt * TT + TT],
                             start=(cd == 0), stop=(cd == NC - 1), waits=wv,
                             sig=(cd == NC - 1))
            ev_ogrp.append(ev_mm)
            xf = c.XF[:, oc * T + tt * TT: oc * T + tt * TT + TT]
            ev_res = p.stt("dve", xf, ops, bp[:, oc:oc + 1], xf, ALU.add, ALU.add,
                           waits=[ev_mm, e_bp], sig=True)
    return ev_res




def emit_l0attn(c, p, w, dep_evs):
    nc = c.nc
    ev_ln = emit_ln_full(c, p, dep_evs)
    kin = nc.dram_tensor("l0a_kin", [1024, 2048], BF16, kind="Internal").ap()
    kout = nc.dram_tensor("l0a_kout", [2048, 2048], BF16, kind="Internal").ap()
    vin = nc.dram_tensor("l0a_vin", [2048, 1024], BF16, kind="Internal").ap()
    vout = nc.dram_tensor("l0a_vout", [4096, 1024], BF16, kind="Internal").ap()

    p.memset("gp", c.CONSTF[:, 8:9], 0.08838834764831845)
    ev_qs = p.memset("gp", c.CONSTF[:, 9:10], 1.0, sig=True)
    qscale = c.CONSTF[:, 8:9]
    DT = c.SCR32[:, 512:640]
    ccol = c.SCR32[:, 0:512]
    e1 = p.op("gp", lambda g: g.iota(DT, pattern=[[-1, 128]], base=0,
                                     channel_multiplier=1,
                                     allow_small_or_imprecise_dtypes=True),
              waits=[ev_ln], sig=True)
    e2 = p.op("gp", lambda g: g.iota(ccol, pattern=[[128, 16], [-128, 32]], base=0,
                                     channel_multiplier=0,
                                     allow_small_or_imprecise_dtypes=True),
              waits=[ev_ln], sig=True)
    ev_cc = p.ts("dve", ccol, ccol, c.META[:, 0:1], None, ALU.add,
                 waits=[e2, c.ev_meta], sig=True)
    e_d = p.dma("sp", c.SMALL[0:3, 104:112], w["dirs"])
    e_h = p.dma("sp", c.SMALL[:, 112:120], w["hs_rep"])
    ds_f = c.SMALL[0:3, 120:128]
    ev = p.tt("dve", ds_f, c.SMALL[0:3, 104:112], c.SMALL[0:3, 112:120], ALU.mult,
              waits=[e_d, e_h], sig=True)
    ev = p.ts("dve", ds_f, ds_f, 11.313708498984761, None, ALU.mult, waits=[ev], sig=True)
    dirs_s = c.SQ[0:3, 2032:2040]
    dirs_u = c.SQ[0:3, 2040:2048]
    ev_ds = p.copy("dve", dirs_s, ds_f, waits=[ev], sig=True)
    ev_du = p.copy("dve", dirs_u, c.SMALL[0:3, 104:112], waits=[e_d], sig=True)

    wq = w["w_qkv"].rearrange("(cd pp) o -> pp cd o", pp=128)
    TT = 512
    ev_kdma = []
    ev_vdma = []
    ev_qev = []
    ev_grp = {}
    ev_stage_hist = []
    for oc in range(24):
        wbuf = (c.WA if oc % 2 == 0 else c.WB)[:, 0:1024]
        wwait = [ev_grp[oc - 2]] if oc >= 2 else []
        ev_w = p.dma("gp", wbuf.rearrange("p (cd f) -> p cd f", cd=NC),
                     wq[:, :, oc * 128:(oc + 1) * 128], waits=wwait)
        for tt in range(4):
            ops = (c.PS_B if (oc * 4 + tt) % 2 == 0 else c.PS_C)[:, 0:512]
            ev_mm = None
            for cd in range(NC):
                wv = []
                if cd == 0:
                    wv = [ev_w, ev_ln]
                    if len(ev_stage_hist) >= 2:
                        wv.append(ev_stage_hist[-2])
                ev_mm = p.mm(ops, wbuf[:, cd * 128:(cd + 1) * 128],
                             _xn_tile(c, cd, tt * TT, TT),
                             start=(cd == 0), stop=(cd == NC - 1), waits=wv,
                             sig=(cd == NC - 1))
            ev_grp[oc] = ev_mm
            if oc < 8:
                dst = c.BIG0[:, oc * T + tt * TT: oc * T + tt * TT + TT]
                e = p.act(dst, ops, AF.Identity, bias=0.0, scale=qscale,
                          waits=[ev_mm, ev_qs], sig=True)
                ev_qev.append(e)
                ev_stage_hist.append(e)
            elif oc < 16:
                st = c.BIG1[:, (tt % 4) * 512:(tt % 4) * 512 + 512]
                e = p.act(st, ops, AF.Identity, bias=0.0,
                          waits=[ev_mm] + ([ev_kdma[-4]] if len(ev_kdma) >= 4 else []),
                          sig=True)
                ev_stage_hist.append(e)
                ed = p.dma("sp", kin[(oc - 8) * 128:(oc - 7) * 128,
                                     tt * TT:(tt + 1) * TT], st, waits=[e])
                ev_kdma.append(ed)
            else:
                cv = oc - 16
                st = c.BIG1[:, 2048 + (tt % 2) * 512: 2048 + (tt % 2) * 512 + 512]
                e = p.act(st, ops, AF.Identity, bias=0.0,
                          waits=[ev_mm] + ([ev_vdma[-2]] if len(ev_vdma) >= 2 else []),
                          sig=True)
                ev_stage_hist.append(e)
                stt_ = c.BIG1[:, 4096 + (tt % 2) * 512: 4096 + (tt % 2) * 512 + 512]
                evs = []
                for j in range(4):
                    pst = (c.PS_T if j % 2 == 0 else c.PS_U)[:, 0:128]
                    evt = p.transpose(pst, st[:, j * 128:(j + 1) * 128], c.id128,
                                      waits=[e] + ([evs[-2]] if len(evs) >= 2 else []),
                                      sig=True)
                    ee = p.copy("dve", stt_[:, j * 128:(j + 1) * 128], pst,
                                waits=[evt], sig=True)
                    evs.append(ee)
                vap = vin[tt * TT:(tt + 1) * TT, cv * 128:(cv + 1) * 128] \
                    .rearrange("(j p) d -> p j d", p=128)
                ed = p.dma("sp", vap,
                           stt_.rearrange("p (j d) -> p j d", j=4), waits=[evs[-1]])
                ev_vdma.append(ed)
    ev_agk = p.cc_allgather(kin, kout, waits=ev_kdma)
    ev_agv = p.cc_allgather(vin, vout, waits=ev_vdma)

    nmx_off = 96
    ev_ctx_ev = []
    ev_score_last = {}
    ev_exp_hist = []
    ev_red_hist = []
    ev_tr_hist = []
    ev_avmm_hist = []
    ev_kp = {}
    for h in range(8):
        kb = c.WA[:, 0:4096]
        vb = c.WB[:, 0:4128]
        wv = [ev_agk] + ([ev_score_last[h - 1]] if h - 1 in ev_score_last else [])
        ev_kl = {}
        ev_vl = {}
        for r in range(2):
            ev_kl[r] = p.dma("gp", kb[:, r * 2048:(r + 1) * 2048],
                             kout[r * 1024 + h * 128: r * 1024 + (h + 1) * 128, :],
                             waits=wv)
            src = vout[r * 2048:(r + 1) * 2048, h * 128:(h + 1) * 128]
            vb3 = vb[:, r * 2064: r * 2064 + 2064].rearrange("p (j d) -> p j d", d=129)
            ev_vl[r] = p.dma("gp", vb3[:, :, 0:128],
                             src.rearrange("(j p) d -> p j d", p=128),
                             waits=[ev_agv] + ([ev_score_last[h - 1]] if h - 1 in ev_score_last else []))
            ev_vl[r] = p.memset("gp", vb3[:, :, 128:129], 1.0,
                                waits=[ev_vl[r]], sig=True)
        kp_sb = c.BIG1[0:1, 8192:12288]
        prev_kp = None
        for r in range(2):
            for j in range(4):
                pr = c.PS_C[0:1, 0:512]
                evm = p.mm(pr, dirs_u[:, h:h + 1], kb[0:3, r * 2048 + j * 512: r * 2048 + j * 512 + 512],
                           start=True, stop=True,
                           waits=[ev_kl[r], ev_du] +
                                 ([prev_kp] if prev_kp else []) +
                                 ([ev_ctx_ev[-1]] if ev_ctx_ev else []),
                           sig=True)
                prev_kp = p.copy("dve", kp_sb[0:1, r * 2048 + j * 512:
                                              r * 2048 + j * 512 + 512],
                                 pr, waits=[evm], sig=True)
                ev_kp[(h, r, j)] = prev_kp
        for qt in range(16):
            qsl = c.BIG0[:, h * T + qt * 128: h * T + qt * 128 + 128]
            bank = c.PS_B if (h * 16 + qt) % 2 == 0 else c.PS_C
            ctxps = bank[:, 0:129]
            qpps = bank[0:1, 256:384]
            ev_qp = p.mm(qpps, dirs_s[:, h:h + 1], qsl[0:3, :], start=True, stop=True,
                         waits=[ev_ds, ev_qev[-1], ev_kp[(h, 1, 3)]] +
                               ([ev_ctx_ev[-2]] if len(ev_ctx_ev) >= 2 else []),
                         sig=True)
            qp_sb = c.SQ[0:1, 1024 + (qt % 2) * 128: 1024 + (qt % 2) * 128 + 128]
            ev_qpc = p.copy("dve", qp_sb, qpps, waits=[ev_qp] +
                            ([ev_ctx_ev[-1]] if ev_ctx_ev else []),
                            sig=True)
            for ch in range(4):
                psa_ = c.PS_A[:, (ch % 2) * 1024:(ch % 2) * 1024 + 1024]
                ev_mm = None
                for j in range(2):
                    wv = []
                    if j == 0:
                        wv = [ev_kl[ch // 2], ev_qev[-1], ev_qpc]
                        if len(ev_red_hist) >= 2:
                            wv.append(ev_red_hist[-2])
                        if ch < 2 and len(ev_exp_hist) >= 1:
                            wv.append(ev_exp_hist[-1])
                    kslice = kb[:, (ch // 2) * 2048 + (ch % 2) * 1024 + j * 512:
                                (ch // 2) * 2048 + (ch % 2) * 1024 + j * 512 + 512]
                    p.mm(psa_[:, j * 512:(j + 1) * 512], qsl, kslice,
                         start=True, stop=False, waits=wv)
                    kpsl = kp_sb[0:1, (ch // 2) * 2048 + (ch % 2) * 1024 + j * 512:
                                 (ch // 2) * 2048 + (ch % 2) * 1024 + j * 512 + 512]
                    ev_mm = p.mm(psa_[:, j * 512:(j + 1) * 512], qp_sb, kpsl,
                                 start=False, stop=True,
                                 waits=[ev_kp[(h, ch // 2, (ch % 2) * 2 + j)]],
                                 sig=(j == 1))
                e = p.red("dve", c.SMALL[:, nmx_off + ch:nmx_off + ch + 1], psa_,
                          ALU.max, negate=True, waits=[ev_mm], sig=True)
                ev_red_hist.append(e)
            nm = c.SMALL[:, nmx_off + 4:nmx_off + 5]
            p.tt("dve", nm, c.SMALL[:, nmx_off:nmx_off + 1],
                 c.SMALL[:, nmx_off + 1:nmx_off + 2], ALU.min)
            p.tt("dve", nm, nm, c.SMALL[:, nmx_off + 2:nmx_off + 3], ALU.min)
            ev_nm = p.tt("dve", nm, nm, c.SMALL[:, nmx_off + 3:nmx_off + 4],
                         ALU.min, sig=True)
            navm = 0
            for ch in range(4):
                psa_ = c.PS_A[:, (ch % 2) * 1024:(ch % 2) * 1024 + 1024]
                ev_mm = None
                for j in range(2):
                    wv = []
                    if j == 0:
                        wv = [ev_nm]
                        if len(ev_exp_hist) >= 2:
                            wv.append(ev_exp_hist[-2])
                        if ch == 0:
                            wv.append(ev_red_hist[-1])
                    kslice = kb[:, (ch // 2) * 2048 + (ch % 2) * 1024 + j * 512:
                                (ch // 2) * 2048 + (ch % 2) * 1024 + j * 512 + 512]
                    p.mm(psa_[:, j * 512:(j + 1) * 512], qsl, kslice,
                         start=True, stop=False, waits=wv)
                    kpsl = kp_sb[0:1, (ch // 2) * 2048 + (ch % 2) * 1024 + j * 512:
                                 (ch // 2) * 2048 + (ch % 2) * 1024 + j * 512 + 512]
                    ev_mm = p.mm(psa_[:, j * 512:(j + 1) * 512], qp_sb, kpsl,
                                 start=False, stop=True, sig=(j == 1))
                pb = c.BIG1[:, ch * 1024:(ch + 1) * 1024]
                ev_exp = p.act(pb, psa_, AF.Exp, bias=nm,
                               waits=[ev_mm, ev_nm], sig=True)
                ev_exp_hist.append(ev_exp)
                for j in range(8):
                    pst = (c.PS_T if j % 2 == 0 else c.PS_U)[:, 128:256]
                    wv = [ev_exp]
                    if len(ev_tr_hist) >= 2:
                        wv.append(ev_tr_hist[-2])
                    if ch == 0 and j == 0 and ev_ctx_ev:
                        wv.append(ev_ctx_ev[-1])
                    evt = p.transpose(pst, pb[:, j * 128:(j + 1) * 128], c.id128,
                                      waits=wv, sig=True)
                    pts = c.BIG1[:, 4096 + (j % 2) * 128: 4096 + (j % 2) * 128 + 128]
                    chj = ch * 8 + j
                    ee = p.stt("dve", pts, DT, ccol[:, qt * 32 + chj: qt * 32 + chj + 1],
                               pst, ALU.is_le, ALU.mult,
                               waits=[evt, ev_cc, e1] +
                               ([ev_avmm_hist[-2]] if len(ev_avmm_hist) >= 2 else []),
                               sig=True)
                    if c.dbg is not None and h == 0 and qt == 0 and ch == 0 and j == 0:
                        p.dma("gp", c.dbg[:, 0:128], pts, waits=[ee])
                        p.dma("gp", c.dbg[:, 128:256], pb[:, 0:128], waits=[ee])
                    ev_tr_hist.append(ee)
                    r = ch // 2
                    j2 = (ch % 2) * 8 + j
                    wv2 = [ee, ev_vl[r]]
                    if navm == 0 and len(ev_ctx_ev) >= 2:
                        wv2.append(ev_ctx_ev[-2])
                    ev_av = p.mm(ctxps, pts,
                                 vb[:, r * 2064 + j2 * 129: r * 2064 + j2 * 129 + 129],
                                 start=(navm == 0), stop=(navm == 31), waits=wv2,
                                 sig=True)
                    ev_avmm_hist.append(ev_av)
                    navm += 1
            lsum = c.SMALL[:, nmx_off + 9:nmx_off + 10]
            ev_ls = p.copy("dve", lsum, ctxps[:, 128:129], waits=[ev_avmm_hist[-1]], sig=True)
            linv = c.SMALL[:, nmx_off + 10:nmx_off + 11]
            ev_li = emit_recip(c, p, linv, lsum, waits=[ev_ls])
            ctx_bf = c.SQ[:, 0:128]
            ev_ce = p.ts("dve", ctx_bf, ctxps[:, 0:128], linv[:, 0:1], None, ALU.mult,
                         waits=[ev_li, ev_avmm_hist[-1]], sig=True)
            evt = p.transpose(c.PS_T[:, 256:384], ctx_bf, c.id128,
                              waits=[ev_ce], sig=True)
            dst = c.XN[:, h * T + qt * 128: h * T + qt * 128 + 128]
            ev_cf = p.copy("dve", dst, c.PS_T[:, 256:384], waits=[evt], sig=True)
            ev_ctx_ev.append(ev_cf)
        ev_score_last[h] = ev_avmm_hist[-1]
    wp = w["w_out"].rearrange("(cd pp) o -> pp cd o", pp=128)
    ev_res = None
    ev_ogrp = []
    for oc in range(NC):
        wbuf = (c.WA if oc % 2 == 0 else c.WB)[:, 0:1024]
        wwait = [ev_ctx_ev[-1]] if oc < 2 else [ev_ogrp[(oc - 2) * 4 + 3]]
        if oc < 2:
            wwait.append(ev_score_last[7])
        ev_w = p.dma("gp", wbuf.rearrange("p (cd f) -> p cd f", cd=NC),
                     wp[:, :, oc * 128:(oc + 1) * 128], waits=wwait)
        for tt in range(4):
            ops = (c.PS_B if (oc * 4 + tt) % 2 == 0 else c.PS_C)[:, 0:512]
            ev_mm = None
            for cd in range(NC):
                wv = []
                if cd == 0:
                    wv = [ev_w, ev_ctx_ev[-1]]
                    if ev_res is not None:
                        wv.append(ev_res)
                ev_mm = p.mm(ops, wbuf[:, cd * 128:(cd + 1) * 128],
                             c.XN[:, cd * T + tt * TT: cd * T + tt * TT + TT],
                             start=(cd == 0), stop=(cd == NC - 1), waits=wv,
                             sig=(cd == NC - 1))
            ev_ogrp.append(ev_mm)
            xf = c.XF[:, oc * T + tt * TT: oc * T + tt * TT + TT]
            ev_res = p.tt("dve", xf, xf, ops, ALU.add, waits=[ev_mm], sig=True)
    return ev_res




def emit_l1ffn(c, p, w, dep_evs, tag):
    nc = c.nc
    ev_ln = emit_ln_full(c, p, dep_evs)
    kin = nc.dram_tensor(f"{tag}_kin", [1024, 2048], BF16, kind="Internal").ap()
    kout = nc.dram_tensor(f"{tag}_kout", [2048, 2048], BF16, kind="Internal").ap()
    vin = nc.dram_tensor(f"{tag}_vin", [2048, 1024], BF16, kind="Internal").ap()
    vout = nc.dram_tensor(f"{tag}_vout", [4096, 1024], BF16, kind="Internal").ap()
    oacc = nc.dram_tensor(f"{tag}_oacc", [1024, 2048], F32, kind="Internal").ap()

    e_rs = p.dma("sp", c.SMALL[:, 352:353], w["rs_rep"])
    ev_gate = []
    for gi, wname in [(0, "svT"), (1, "soT")]:
        sw = c.WA[:, gi * 512:(gi + 1) * 512]
        e_sw = p.dma("gp", sw.rearrange("p (cd e) -> p cd e", cd=NC),
                     w[wname].rearrange("(cd pp) e -> pp cd e", pp=128),
                     waits=[ev_ln] if gi == 0 else [])
        probs = c.SMALL[:, 192:256]
        probs3 = probs.rearrange("p (h e) -> p h e", h=8)
        v1 = c.SMALL[:, 256:264]
        scr2 = c.SMALL[:, 264:328]
        scr3 = scr2.rearrange("p (h e) -> p h e", h=8)
        v2 = c.SMALL[:, 328:336]
        s2a = c.SCR32[:, 1024 + gi * 128: 1024 + gi * 128 + 128]
        for tt in range(16):
            ps = (c.PS_B if tt % 2 == 0 else c.PS_C)[:, 0:64]
            ev = None
            for cd in range(NC):
                wv = []
                if cd == 0:
                    wv = [e_sw, ev_ln]
                    if ev_gate:
                        wv.append(ev_gate[-1])
                ev = p.mm(ps, _xn_tile(c, cd, tt * 128, 128), sw[:, cd * 64:(cd + 1) * 64],
                          start=(cd == 0), stop=(cd == NC - 1), waits=wv,
                          sig=(cd == NC - 1))
            ev = p.act(probs, ps, AF.Sigmoid, waits=[ev] + ([ev_gate[-1]] if ev_gate else []),
                       sig=True)
            v1b = v1.rearrange("p (h o) -> p h o", o=1).broadcast_to([128, 8, 8])
            v2b = v2.rearrange("p (h o) -> p h o", o=1).broadcast_to([128, 8, 8])
            p.red("dve", v1, probs3, ALU.max, waits=[ev])
            p.tt("dve", scr3, probs3, v1b, ALU.is_equal)
            p.stt("dve", scr3, scr3, -1e30, probs3, ALU.mult, ALU.add)
            p.red("dve", v2, scr3, ALU.max)
            p.tt("dve", scr3, probs3, v2b, ALU.is_ge)
            p.tt("dve", c.SCR32[:, 2048 + gi * 1024 + tt * 64:
                                 2048 + gi * 1024 + tt * 64 + 64]
                 .rearrange("p (h e) -> p h e", h=8),
                 probs3, scr3, ALU.mult)
            ev = p.tt("dve", s2a[:, tt * 8:(tt + 1) * 8], v1, v2, ALU.add, sig=True)
            ev_gate.append(ev)
        ev = p.ts("dve", s2a, s2a, 1e-9, None, ALU.max, waits=[ev_gate[-1]], sig=True)
        ev = emit_recip(c, p, s2a, s2a, waits=[ev])
        ev = p.ts("dve", s2a, s2a, c.SMALL[:, 352:353], None, ALU.mult,
                  waits=[ev, e_rs], sig=True)
        g4 = c.SCR32[:, 2048 + gi * 1024: 2048 + (gi + 1) * 1024] \
            .rearrange("p (tt h e) -> p tt h e", tt=16, h=8)
        gb4 = s2a.rearrange("p (tt h) -> p tt h", tt=16) \
            .rearrange("p tt (h o) -> p tt h o", o=1).broadcast_to([128, 16, 8, 8])
        ev = p.tt("dve", g4, g4, gb4, ALU.mult, waits=[ev], sig=True)
        ev_gate.append(ev)
    ev_gates = ev_gate[-1]

    TT = 512
    ev_kdma = []
    ev_qev = []
    ev_grp = {}
    ev_stage_hist = []
    for oc in range(16):
        wsrc = w["qwT"] if oc < 8 else w["kwT"]
        occ = oc % 8
        wbuf = c.WB[:, 0:1024] if oc % 2 == 0 else c.WB[:, 1024:2048]
        wwait = [ev_grp[oc - 2]] if oc >= 2 else [ev_gates]
        ev_w = p.dma("gp", wbuf.rearrange("p (cd f) -> p cd f", cd=NC),
                     wsrc.rearrange("(cd pp) o -> pp cd o", pp=128)[:, :, occ * 128:(occ + 1) * 128],
                     waits=wwait)
        for tt in range(4):
            ops = (c.PS_B if (oc * 4 + tt) % 2 == 0 else c.PS_C)[:, 0:512]
            ev_mm = None
            for cd in range(NC):
                wv = []
                if cd == 0:
                    wv = [ev_w, ev_ln, ev_gates]
                    if len(ev_stage_hist) >= 2:
                        wv.append(ev_stage_hist[-2])
                ev_mm = p.mm(ops, wbuf[:, cd * 128:(cd + 1) * 128],
                             _xn_tile(c, cd, tt * TT, TT),
                             start=(cd == 0), stop=(cd == NC - 1), waits=wv,
                             sig=(cd == NC - 1))
            ev_grp[oc] = ev_mm
            if oc < 8:
                dst = c.BIG0[:, occ * T + tt * TT: occ * T + tt * TT + TT]
                e = p.act(dst, ops, AF.Identity, bias=0.0, scale=c.rsqrt128_col,
                          waits=[ev_mm, c.ev_consts], sig=True)
                ev_qev.append(e)
                ev_stage_hist.append(e)
            else:
                st = c.BIG1[:, (tt % 4) * 512:(tt % 4) * 512 + 512]
                e = p.act(st, ops, AF.Identity, bias=0.0,
                          waits=[ev_mm] + ([ev_kdma[-4]] if len(ev_kdma) >= 4 else []),
                          sig=True)
                ev_stage_hist.append(e)
                ed = p.dma("sp", kin[occ * 128:(occ + 1) * 128,
                                     tt * TT:(tt + 1) * TT], st, waits=[e])
                ev_kdma.append(ed)
    ev_agk = p.cc_allgather(kin, kout, waits=ev_kdma)

    ev_vdma = []
    ev_xv_hist = []
    for h in range(8):
        acc = c.SCR32[:, 0:2048]
        for e in range(8):
            wbuf = c.WB[:, 2048:3072] if e % 2 == 0 else c.WB[:, 3072:4096]
            wwait = [ev_xv_hist[-16]] if len(ev_xv_hist) >= 16 else [ev_grp[15]]
            ev_w = p.dma("gp", wbuf.rearrange("p (cd f) -> p cd f", cd=NC),
                         w["vw"].rearrange("he (cd pp) f -> pp he cd f", pp=128)
                         [:, h * 8 + e, :, :], waits=wwait)
            for tt in range(16):
                ps = (c.PS_B if (e * 16 + tt) % 2 == 0 else c.PS_C)[:, 0:128]
                ev_mm = None
                for cd in range(NC):
                    wv = []
                    if cd == 0:
                        wv = [ev_w, ev_gates]
                        if len(ev_xv_hist) >= 2:
                            wv.append(ev_xv_hist[-2])
                        if h > 0 and e == 0 and tt == 0:
                            wv.append(ev_vdma[-1])
                    ev_mm = p.mm(ps, _xn_tile(c, cd, tt * 128, 128),
                                 wbuf[:, cd * 128:(cd + 1) * 128],
                                 start=(cd == 0), stop=(cd == NC - 1), waits=wv,
                                 sig=(cd == NC - 1))
                gcol = c.SCR32[:, 2048 + tt * 64 + h * 8 + e:
                               2048 + tt * 64 + h * 8 + e + 1]
                dst = acc[:, tt * 128:(tt + 1) * 128]
                if e == 0:
                    ev2 = p.ts("dve", dst, ps, gcol, None, ALU.mult,
                               waits=[ev_mm, ev_gates], sig=True)
                else:
                    ev2 = p.stt("dve", dst, ps, gcol, dst, ALU.mult, ALU.add,
                                waits=[ev_mm], sig=True)
                ev_xv_hist.append(ev2)
        stg = c.BIG1[:, 2048:4096]
        ev3 = p.copy("dve", stg, acc, waits=[ev_xv_hist[-1]] +
                     ([ev_vdma[-1]] if ev_vdma else []), sig=True)
        ed = p.dma("sp", vin.rearrange("(tt part) d -> part tt d", part=128)
                   [:, :, h * 128:(h + 1) * 128],
                   stg.rearrange("p (tt d) -> p tt d", d=128), waits=[ev3])
        ev_vdma.append(ed)
    ev_agv = p.cc_allgather(vin, vout, waits=ev_vdma)

    nmx_off = 96
    ev_ctx_ev = []
    ev_score_last = {}
    ev_exp_hist = []
    ev_red_hist = []
    ev_tr_hist = []
    ev_avmm_hist = []
    for h in range(8):
        kb = c.WA[:, 0:4096]
        vb = c.WB[:, 0:4128]
        wv = [ev_agk] + ([ev_score_last[h - 1]] if h - 1 in ev_score_last else [])
        if h == 0:
            wv.append(ev_gate[-1])
        ev_kl = {}
        ev_vl = {}
        for r in range(2):
            ev_kl[r] = p.dma("gp", kb[:, r * 2048:(r + 1) * 2048],
                             kout[r * 1024 + h * 128: r * 1024 + (h + 1) * 128, :],
                             waits=wv)
            src = vout[r * 2048:(r + 1) * 2048, h * 128:(h + 1) * 128]
            vb3 = vb[:, r * 2064: r * 2064 + 2064].rearrange("p (j d) -> p j d", d=129)
            ev = p.dma("gp", vb3[:, :, 0:128],
                       src.rearrange("(j p) d -> p j d", p=128),
                       waits=[ev_agv] + ([ev_score_last[h - 1]] if h - 1 in ev_score_last else []))
            ev_vl[r] = p.memset("gp", vb3[:, :, 128:129], 1.0, waits=[ev], sig=True)
        for qt in range(16):
            qsl = c.BIG0[:, h * T + qt * 128: h * T + qt * 128 + 128]
            bank = c.PS_B if (h * 16 + qt) % 2 == 0 else c.PS_C
            ctxps = bank[:, 0:129]
            for ch in range(4):
                psa_ = c.PS_A[:, (ch % 2) * 1024:(ch % 2) * 1024 + 1024]
                ev_mm = None
                for j in range(2):
                    wvv = []
                    if j == 0:
                        wvv = [ev_kl[ch // 2], ev_qev[-1]]
                        if len(ev_red_hist) >= 2:
                            wvv.append(ev_red_hist[-2])
                        if ch < 2 and len(ev_exp_hist) >= 1:
                            wvv.append(ev_exp_hist[-1])
                    ev_mm = p.mm(psa_[:, j * 512:(j + 1) * 512], qsl,
                                 kb[:, (ch // 2) * 2048 + (ch % 2) * 1024 + j * 512:
                                    (ch // 2) * 2048 + (ch % 2) * 1024 + j * 512 + 512],
                                 start=True, stop=True, waits=wvv, sig=(j == 1))
                e = p.red("dve", c.SMALL[:, nmx_off + ch:nmx_off + ch + 1], psa_,
                          ALU.max, negate=True, waits=[ev_mm], sig=True)
                ev_red_hist.append(e)
            nm = c.SMALL[:, nmx_off + 4:nmx_off + 5]
            p.tt("dve", nm, c.SMALL[:, nmx_off:nmx_off + 1],
                 c.SMALL[:, nmx_off + 1:nmx_off + 2], ALU.min)
            p.tt("dve", nm, nm, c.SMALL[:, nmx_off + 2:nmx_off + 3], ALU.min)
            ev_nm = p.tt("dve", nm, nm, c.SMALL[:, nmx_off + 3:nmx_off + 4],
                         ALU.min, sig=True)
            navm = 0
            for ch in range(4):
                psa_ = c.PS_A[:, (ch % 2) * 1024:(ch % 2) * 1024 + 1024]
                ev_mm = None
                for j in range(2):
                    wvv = []
                    if j == 0:
                        wvv = [ev_nm]
                        if len(ev_exp_hist) >= 2:
                            wvv.append(ev_exp_hist[-2])
                        if ch == 0:
                            wvv.append(ev_red_hist[-1])
                    ev_mm = p.mm(psa_[:, j * 512:(j + 1) * 512], qsl,
                                 kb[:, (ch // 2) * 2048 + (ch % 2) * 1024 + j * 512:
                                    (ch // 2) * 2048 + (ch % 2) * 1024 + j * 512 + 512],
                                 start=True, stop=True, waits=wvv, sig=(j == 1))
                pb = c.BIG1[:, ch * 1024:(ch + 1) * 1024]
                ev_exp = p.act(pb, psa_, AF.Exp, bias=nm, waits=[ev_mm, ev_nm], sig=True)
                ev_exp_hist.append(ev_exp)
                for j in range(8):
                    pst = (c.PS_T if j % 2 == 0 else c.PS_U)[:, 128:256]
                    wvv = [ev_exp]
                    if len(ev_tr_hist) >= 2:
                        wvv.append(ev_tr_hist[-2])
                    if ch == 0 and j == 0 and ev_ctx_ev:
                        wvv.append(ev_ctx_ev[-1])
                    evt = p.transpose(pst, pb[:, j * 128:(j + 1) * 128], c.id128,
                                      waits=wvv, sig=True)
                    pts = c.BIG1[:, 4096 + (j % 2) * 128: 4096 + (j % 2) * 128 + 128]
                    ee = p.copy("dve", pts, pst, waits=[evt] +
                                ([ev_avmm_hist[-2]] if len(ev_avmm_hist) >= 2 else []),
                                sig=True)
                    ev_tr_hist.append(ee)
                    r = ch // 2
                    j2 = (ch % 2) * 8 + j
                    wv2 = [ee, ev_vl[r]]
                    if navm == 0 and len(ev_ctx_ev) >= 2:
                        wv2.append(ev_ctx_ev[-2])
                    ev_av = p.mm(ctxps, pts,
                                 vb[:, r * 2064 + j2 * 129: r * 2064 + j2 * 129 + 129],
                                 start=(navm == 0), stop=(navm == 31), waits=wv2,
                                 sig=True)
                    ev_avmm_hist.append(ev_av)
                    navm += 1
            lsum = c.SMALL[:, nmx_off + 9:nmx_off + 10]
            ev_ls = p.copy("dve", lsum, ctxps[:, 128:129], waits=[ev_avmm_hist[-1]], sig=True)
            linv = c.SMALL[:, nmx_off + 10:nmx_off + 11]
            ev_li = emit_recip(c, p, linv, lsum, waits=[ev_ls])
            dst = c.XN[:, h * T + qt * 128: h * T + qt * 128 + 128]
            ev_cf = p.ts("dve", dst, ctxps[:, 0:128], linv[:, 0:1], None, ALU.mult,
                         waits=[ev_li, ev_avmm_hist[-1]], sig=True)
            ev_ctx_ev.append(ev_cf)
        ev_score_last[h] = ev_avmm_hist[-1]

    if c.dbg2 is not None:
        p.dma("gp", c.dbg2[:, 0:2048], c.SCR32[:, 2048:4096], waits=[ev_gates])
        p.dma("gp", c.dbg2[:, 2048:2048 + 16384 - 2048], c.XN[:, 0:14336],
              waits=[ev_ctx_ev[-1]])
    ev_acc_dma = []
    ev_out_hist = []
    for h in range(8):
        ev_ow = []
        for e in range(8):
            wwv = [ev_score_last[7], ev_ctx_ev[-1]] if h == 0 else                 ([ev_out_hist[-1]] if ev_out_hist else [])
            ev_ow.append(p.dma("gp", c.BIG0[:, e * 1024:(e + 1) * 1024],
                               w["ow"].rearrange("he pp d -> pp he d")[:, h * 8 + e, :],
                               waits=wwv))
        for qt in range(16):
            ctx_t = c.XN[:, h * T + qt * 128: h * T + qt * 128 + 128]
            ev_ge = []
            for e in range(8):
                gcol = c.SCR32[:, 3072 + qt * 64 + h * 8 + e:
                               3072 + qt * 64 + h * 8 + e + 1]
                cge = c.BIG1[:, 6144:6272]
                evg = p.ts("dve", cge, ctx_t, gcol, None, ALU.mult,
                           waits=[ev_ctx_ev[-1], ev_gates] +
                           ([ev_ge[-1]] if ev_ge else
                            ([ev_out_hist[-1]] if ev_out_hist else [])),
                           sig=True)
                pst = (c.PS_T if e % 2 == 0 else c.PS_U)[:, 384:512]
                evt = p.transpose(pst, cge, c.id128,
                                  waits=[evg] + ([ev_ge[-2]] if len(ev_ge) >= 2 else []),
                                  sig=True)
                ee = p.copy("dve", c.BIG1[:, 8192 + e * 128: 8192 + e * 128 + 128],
                            pst, waits=[evt] + ([ev_out_hist[-1]] if ev_out_hist else []),
                            sig=True)
                ev_ge.append(ee)
            stg = c.SCR32[:, 0:1024]
            ev_ev = []
            for dch in range(NC):
                off = (dch % 4) * 512 + (dch // 4) * 128
                ops = c.PS_A[:, off:off + 128]
                ev_mm = None
                for e in range(8):
                    wvv = []
                    if e == 0:
                        wvv = [ev_ow[-1], ev_ge[-1]]
                        if len(ev_ev) >= 4:
                            wvv.append(ev_ev[-4])
                        if dch == 0 and len(ev_out_hist) >= 1:
                            wvv.append(ev_out_hist[-1])
                    ev_mm = p.mm(ops, c.BIG0[:, e * 1024 + dch * 128: e * 1024 + dch * 128 + 128],
                                 c.BIG1[:, 8192 + e * 128: 8192 + e * 128 + 128],
                                 start=(e == 0), stop=(e == 7), waits=wvv,
                                 sig=(e == 7))
                ev2 = p.copy("dve", stg[:, dch * 128:(dch + 1) * 128], ops,
                             waits=[ev_mm], sig=True)
                ev_ev.append(ev2)
            ev_out_hist.append(ev_ev[-1])
            ed = p.dma("gp", oacc.rearrange("(cd pp) t -> pp cd t", pp=128)
                       [:, :, qt * 128:(qt + 1) * 128],
                       stg.rearrange("p (cd t) -> p cd t", cd=NC),
                       waits=[ev_ev[-1]], ch=7,
                       **({} if h == 0 else {"accum_op": ALU.add}))
            ev_acc_dma.append(ed)
    ev_res = None
    for tt in range(4):
        lw = [ev_acc_dma[-1]]
        ev_l = p.dma("sp", c.SCR32[:, 0:NC * TT].rearrange("p (cd s) -> p cd s", cd=NC),
                     oacc.rearrange("(cd pp) t -> pp cd t", pp=128)
                     [:, :, tt * TT:(tt + 1) * TT],
                     waits=lw + ([ev_res] if ev_res else []))
        xf3 = _chunk3(c.XF, T, tt * TT, TT)
        a3 = c.SCR32[:, 0:NC * TT].rearrange("p (cd s) -> p cd s", cd=NC)
        ev_res = p.stt("dve", xf3, a3, 0.5, xf3, ALU.mult, ALU.add,
                       waits=[ev_l], sig=True)
    return ev_res




L1A_ALPHA = 0.42044820762685725
L1A_LIN = 1.1892071150027211
L1A_STOP = {"v": None}


def _l1a_vext(c, tt):
    return c.SQ[:, 0:1040] if tt == 15 else c.BIG0[:, tt * 1040: tt * 1040 + 1040]


def emit_l1attn(c, p, w, dep_evs):
    nc = c.nc
    ev_ln = emit_ln_full(c, p, dep_evs)
    sin = nc.dram_tensor("l1a_sin", [4368, 65], F32, kind="Internal").ap()
    sout = nc.dram_tensor("l1a_sout", [8736, 65], F32, kind="Internal").ap()
    p.memset("gp", c.CONSTF[:, 5:6], 1e-12)
    ev_c11 = p.memset("gp", c.CONSTF[:, 11:12], L1A_ALPHA, sig=True)
    alpha_col = c.CONSTF[:, 11:12]
    eps12_col = c.CONSTF[:, 5:6]

    ev_wq = p.dma("gp", c.WA[:, 0:2048].rearrange("p (cd f) -> p cd f", cd=NC),
                  w["WqT"].rearrange("(cd pp) o -> pp cd o", pp=128))
    ev_wk = p.dma("gp", c.WA[:, 2048:4096].rearrange("p (cd f) -> p cd f", cd=NC),
                  w["WkT"].rearrange("(cd pp) o -> pp cd o", pp=128))
    ev_qev = []
    ev_kev = []
    for tt in range(16):
        for which in range(2):
            ps = (c.PS_B if which == 0 else c.PS_C)[:, 0:256]
            wsl = c.WA[:, which * 2048: which * 2048 + 2048]
            evl = ev_qev if which == 0 else ev_kev
            ev_mm = None
            for cd in range(NC):
                wv = []
                if cd == 0:
                    wv = [ev_wq if which == 0 else ev_wk, ev_ln]
                    if evl:
                        wv.append(evl[-1])
                ev_mm = p.mm(ps, _xn_tile(c, cd, tt * 128, 128),
                             wsl[:, cd * 256:(cd + 1) * 256],
                             start=(cd == 0), stop=(cd == NC - 1), waits=wv,
                             sig=(cd == NC - 1))
            dst = c.BIG1[:, which * 4096 + tt * 256: which * 4096 + tt * 256 + 256]
            e = p.act(dst, ps, AF.Identity, bias=0.0, scale=alpha_col,
                      waits=[ev_mm, ev_c11], sig=True)
            evl.append(e)
    ev_ones = None
    for tt in range(16):
        ev_ones = p.memset("gp", _l1a_vext(c, tt).rearrange("p (h e) -> p h e", e=65)
                           [:, :, 64:65], 1.0, waits=[ev_ln], sig=True)
    ev_v = []
    for hp in range(8):
        wsl = c.WA[:, (hp % 2) * 1024:(hp % 2) * 1024 + 1024]
        wwv = [ev_v[-16]] if len(ev_v) >= 16 else [ev_qev[-1], ev_kev[-1]]
        ev_w = p.dma("gp", wsl.rearrange("p (cd f) -> p cd f", cd=NC),
                     w["WvT"].rearrange("(cd pp) o -> pp cd o", pp=128)
                     [:, :, hp * 128:(hp + 1) * 128], waits=wwv)
        for tt in range(16):
            ps = (c.PS_B if tt % 2 == 0 else c.PS_C)[:, 256:384]
            ev_mm = None
            for cd in range(NC):
                wv = []
                if cd == 0:
                    wv = [ev_w, ev_qev[-1], ev_kev[-1]]
                    if len(ev_v) >= 2:
                        wv.append(ev_v[-2])
                ev_mm = p.mm(ps, _xn_tile(c, cd, tt * 128, 128),
                             wsl[:, cd * 128:(cd + 1) * 128],
                             start=(cd == 0), stop=(cd == NC - 1), waits=wv,
                             sig=(cd == NC - 1))
            dst = _l1a_vext(c, tt)[:, 2 * hp * 65: 2 * hp * 65 + 130].rearrange(
                "p (h e) -> p h e", e=65)[:, :, 0:64]
            e = p.copy("dve", dst, ps.rearrange("p (h e) -> p h e", e=64),
                       waits=[ev_mm], sig=True)
            ev_v.append(e)
    ev_vd = ev_v[-1]
    if L1A_STOP["v"] == "phaseA":
        return ev_vd

    M128 = c.SQ[:, 1572:1700]
    DTm = c.SCR32[:, 600:728]
    e_m0 = p.op("gp", lambda g: g.iota(DTm, pattern=[[-1, 128]], base=0,
                                       channel_multiplier=1,
                                       allow_small_or_imprecise_dtypes=True),
                waits=[ev_ln], sig=True)
    ev_M = p.ts("dve", M128, DTm, 0.0, None, ALU.is_le, waits=[e_m0, ev_ln], sig=True)
    ev_wbo = p.memset("gp", c.WB[:, 0:4368].rearrange("p (tt e) -> p tt e", e=273)
                      [:, :, 272:273], 1.0, waits=[ev_ln], sig=True)
    e_r0 = p.memset("gp", c.ROWS[0:1, 0:2048], 1.0,
                    waits=[ev_ln] + c.rows_guard, sig=True)
    e_o1 = p.dma("gp", c.WA[16:17, 0:2048], c.ROWS[0:1, 0:2048],
                 waits=[e_r0, ev_v[-1], ev_v[-2]])
    ev_or = p.dma("gp", c.WA[16:17, 2048:4096], c.ROWS[0:1, 0:2048],
                  waits=[e_r0, ev_kev[-1]])
    c.rows_guard = [e_o1, ev_or]
    if L1A_STOP["v"] == "mask":
        return ev_or

    S0 = c.SCR32[:, 0:65]
    S1 = c.SCR32[:, 65:130]
    S2 = c.SCR32[0:17, 130:195]

    def phik_build(h, ev_free, with_fm):
        ev_b = []
        ev_fm = []
        for tt in range(16):
            ksl = c.BIG1[:, 4096 + tt * 256 + h * 16: 4096 + tt * 256 + h * 16 + 16]
            wt = c.WB[:, tt * 273: tt * 273 + 272]
            wv = [ev_kev[-1]] + (ev_free if tt == 0 else [])
            q3a = ksl.rearrange("p (c1 o) -> p c1 o", o=1).broadcast_to([128, 16, 16])
            q3b = ksl.rearrange("p (o c2) -> p o c2", o=1).broadcast_to([128, 16, 16])
            e1 = p.tt("dve", wt[:, 0:256].rearrange("p (c1 c2) -> p c1 c2", c1=16),
                      q3a, q3b, ALU.mult, waits=wv, sig=True)
            e2 = p.ts("dve", wt[:, 256:272], ksl, L1A_LIN, None, ALU.mult,
                      waits=[e1], sig=True)
            ev_b.append(e2)
            if with_fm:
                pst = (c.PS_T if tt % 2 == 0 else c.PS_U)
                wfm = [e2] + ([ev_fm[-4]] if len(ev_fm) >= 6 else [])
                ea = p.transpose(pst[:, 384:512], wt[:, 0:128], c.id128,
                                 waits=wfm, sig=True)
                eb = p.transpose(pst[:, 512:640], wt[:, 128:256], c.id128, sig=True)
                ec = p.transpose(pst[0:16, 256:384], wt[:, 256:272],
                                 c.id128, sig=True)
                f1 = p.copy("dve", c.BIG1[:, 12288 + tt * 128: 12288 + tt * 128 + 128],
                            pst[:, 384:512], waits=[ec], sig=True)
                f2 = p.copy("dve", c.BIG1[:, 14336 + tt * 128: 14336 + tt * 128 + 128],
                            pst[:, 512:640], waits=[ec], sig=True)
                f3 = p.copy("dve", c.WA[0:16, 2048 + tt * 128: 2048 + tt * 128 + 128],
                            pst[0:16, 256:384], waits=[ec, ev_or], sig=True)
                ev_fm += [f1, f2, f3]
        return ev_b, ev_fm

    def state_mms(h, ch, bank_off, sadds, extra_mm_waits):
        po = bank_off + (ch % 2) * 1024
        vt = _l1a_vext(c, ch)[:, h * 65: h * 65 + 65]
        wt = c.WB[:, ch * 273: ch * 273 + 273]
        guard = [sadds[ch - 2][-1]] if ch - 2 in sadds else []
        m1 = p.mm(c.PS_A[:, po: po + 65], wt[:, 0:128], vt,
                  start=True, stop=True, waits=extra_mm_waits + guard, sig=True)
        m2 = p.mm(c.PS_A[:, po + 65: po + 130], wt[:, 128:256], vt,
                  start=True, stop=True, sig=True)
        m3 = p.mm(c.PS_A[0:17, po + 130: po + 195], wt[:, 256:273], vt,
                  start=True, stop=True, sig=True)
        a1 = p.tt("dve", S0, S0, c.PS_A[:, po: po + 65], ALU.add,
                  waits=[m1], sig=True)
        a2 = p.tt("dve", S1, S1, c.PS_A[:, po + 65: po + 130], ALU.add,
                  waits=[m2], sig=True)
        a3 = p.tt("dve", S2, S2, c.PS_A[0:17, po + 130: po + 195], ALU.add,
                  waits=[m3], sig=True)
        return [a1, a2, a3]

    ev_ship = {}
    ev_free = [ev_vd, ev_ones]
    for h in range(16):
        ev_b, _ = phik_build(h, ev_free, with_fm=False)
        if L1A_STOP["v"] == "build0" and h == 0:
            return ev_b[-1]
        z_wv = [ev_ln] + (ev_ship[h - 1] if h - 1 in ev_ship else [])
        z1 = p.memset("gp", c.SCR32[:, 0:195], 0.0, waits=z_wv, sig=True)
        sadds = {}
        for ch in range(16):
            xw = [ev_b[ch], ev_vd, ev_ones, z1, ev_wbo]
            sadds[ch] = state_mms(h, ch, 512, sadds, xw)
        if L1A_STOP["v"] == "sadd0" and h == 0:
            return sadds[15][-1]
        e1 = p.dma("sp", sin[h * 273: h * 273 + 128, :], S0, waits=[sadds[15][0]])
        e2 = p.dma("sp", sin[h * 273 + 128: h * 273 + 256, :], S1, waits=[sadds[15][1]])
        e3 = p.dma("sp", sin[h * 273 + 256: h * 273 + 273, :], S2, waits=[sadds[15][2]])
        ev_ship[h] = [e1, e2, e3]
        ev_free = [ev_b[-1], sadds[15][-1], e1, e2, e3]
    if L1A_STOP["v"] == "state":
        return ev_ship[15][-1]
    ev_ag = p.cc_allgather(sin, sout, waits=[e for v in ev_ship.values() for e in v] +
                           [("sp0", p.counts["sp0"]), ("sp1", p.counts["sp1"]),
                            ("sp2", p.counts["sp2"]), ("sp3", p.counts["sp3"])])
    if L1A_STOP["v"] == "ag":
        return ev_ag

    SB0 = c.SQ[:, 1312:1377]
    SB1 = c.SQ[:, 1377:1442]
    SB2 = c.SQ[0:17, 1442:1507]
    ev_y = []
    ev_am_hist = []
    ev_init_prev = []
    for h in range(16):
        ev_b, ev_fmk = phik_build(h, ev_free, with_fm=True)
        ev_fmq = []
        for tt in range(16):
            qsl = c.BIG1[:, tt * 256 + h * 16: tt * 256 + h * 16 + 16]
            stag = c.SQ[:, 1040:1312] if tt % 2 == 0 else c.SQ[:, 1700:1972]
            wv = [ev_qev[-1]] + ([ev_fmq[-4]] if len(ev_fmq) >= 4 else
                                 ([ev_y[-1]] if ev_y else [ev_M]))
            q3a = qsl.rearrange("p (c1 o) -> p c1 o", o=1).broadcast_to([128, 16, 16])
            q3b = qsl.rearrange("p (o c2) -> p o c2", o=1).broadcast_to([128, 16, 16])
            e1 = p.tt("dve", stag[:, 0:256].rearrange("p (c1 c2) -> p c1 c2", c1=16),
                      q3a, q3b, ALU.mult, waits=wv, sig=True)
            e2 = p.ts("dve", stag[:, 256:272], qsl, L1A_LIN, None, ALU.mult,
                      waits=[e1], sig=True)
            pst = (c.PS_T if tt % 2 == 0 else c.PS_U)
            wfm = [e2] + ([ev_fmq[-4]] if len(ev_fmq) >= 6 else []) + \
                ([ev_fmk[-1]] if tt == 0 else [])
            ea = p.transpose(pst[:, 384:512], stag[:, 0:128], c.id128,
                             waits=wfm, sig=True)
            eb = p.transpose(pst[:, 512:640], stag[:, 128:256], c.id128, sig=True)
            ec = p.transpose(pst[0:16, 256:384], stag[:, 256:272], c.id128, sig=True)
            f1 = p.copy("dve", c.BIG1[:, 8192 + tt * 128: 8192 + tt * 128 + 128],
                        pst[:, 384:512], waits=[ec], sig=True)
            f2 = p.copy("dve", c.BIG1[:, 10240 + tt * 128: 10240 + tt * 128 + 128],
                        pst[:, 512:640], waits=[ec], sig=True)
            f3 = p.copy("dve", c.WA[0:16, tt * 128: tt * 128 + 128],
                        pst[0:16, 256:384], waits=[ec, e_o1], sig=True)
            ev_fmq += [f1, f2, f3]
        r0 = c.SCR32[:, 260:325]
        r1 = c.SCR32[:, 325:390]
        r2 = c.SCR32[0:17, 390:455]
        rwv = [ev_ag] + ev_init_prev
        d0 = p.dma("gp", r0, sout[h * 273: h * 273 + 128, :], waits=rwv)
        d1 = p.dma("gp", r1, sout[h * 273 + 128: h * 273 + 256, :], waits=rwv)
        d2 = p.dma("gp", r2, sout[h * 273 + 256: h * 273 + 273, :], waits=rwv)
        swv = [d0, d1, d2, c.ev_meta]
        i0 = p.ts("dve", S0, r0, c.META[:, 1:2], None, ALU.mult, waits=swv, sig=True)
        i1 = p.ts("dve", S1, r1, c.META[:, 1:2], None, ALU.mult, waits=swv, sig=True)
        i2 = p.ts("dve", S2, r2, c.META[0:17, 1:2], None, ALU.mult, waits=swv, sig=True)
        ev_init_prev = [i0, i1, i2]
        b0 = p.copy("dve", SB0, S0, waits=[i0] + ([ev_y[-1]] if ev_y else []), sig=True)
        b1 = p.copy("dve", SB1, S1, waits=[i1], sig=True)
        b2 = p.copy("dve", SB2, S2, waits=[i2], sig=True)
        ev_sbf = [b0, b1, b2]
        sadds = {}
        for ch in range(16):
            atp = c.PS_A[:, (ch % 2) * 1024:(ch % 2) * 1024 + 128]
            a_wv = [ev_fmk[-1], ev_fmq[-1]] + \
                ([ev_am_hist[-2]] if len(ev_am_hist) >= 2 else [])
            p.mm(atp, c.BIG1[:, 12288 + ch * 128: 12288 + ch * 128 + 128],
                 c.BIG1[:, 8192 + ch * 128: 8192 + ch * 128 + 128],
                 start=True, stop=False, waits=a_wv)
            p.mm(atp, c.BIG1[:, 14336 + ch * 128: 14336 + ch * 128 + 128],
                 c.BIG1[:, 10240 + ch * 128: 10240 + ch * 128 + 128],
                 start=False, stop=False)
            ev_at = p.mm(atp, c.WA[0:17, 2048 + ch * 128: 2048 + ch * 128 + 128],
                         c.WA[0:17, ch * 128: ch * 128 + 128],
                         start=False, stop=True, sig=True)
            ats = c.BIG0[:, 15600 + (ch % 2) * 128: 15600 + (ch % 2) * 128 + 128]
            ev_am = p.tt("dve", ats, atp, M128, ALU.mult,
                         waits=[ev_at, ev_M] + ([ev_y[-2]] if len(ev_y) >= 2 else []),
                         sig=True)
            ev_am_hist.append(ev_am)
            vt = _l1a_vext(c, ch)[:, h * 65: h * 65 + 65]
            yps = (c.PS_B if ch % 2 == 0 else c.PS_C)[:, 0:65]
            p.mm(yps, ats, vt, start=True, stop=False,
                 waits=[ev_am, ev_vd, ev_ones, ev_sbf[-1]] +
                 ([ev_y[-2]] if len(ev_y) >= 2 else []))
            p.mm(yps, c.BIG1[:, 8192 + ch * 128: 8192 + ch * 128 + 128], SB0,
                 start=False, stop=False)
            p.mm(yps, c.BIG1[:, 10240 + ch * 128: 10240 + ch * 128 + 128], SB1,
                 start=False, stop=False)
            ev_yg = p.mm(yps, c.WA[0:17, ch * 128: ch * 128 + 128], SB2,
                         start=False, stop=True, sig=True)
            dcol = c.SCR32[:, 520 + (ch % 2): 521 + (ch % 2)]
            e_d = p.copy("dve", dcol, yps[:, 64:65], waits=[ev_yg], sig=True)
            e_r = p.act(dcol, dcol, AF.Ln, bias=eps12_col, waits=[e_d], sig=True)
            e_r = p.act(dcol, dcol, AF.Exp, scale=c.negone_col,
                        waits=[e_r, c.ev_consts], sig=True)
            dst = c.XN[:, ch * 1024 + h * 64: ch * 1024 + h * 64 + 64]
            e_ye = p.ts("dve", dst, yps[:, 0:64], dcol[:, 0:1], None, ALU.mult,
                        waits=[e_r], sig=True)
            ev_y.append(e_ye)
            if ch < 15:
                sadds[ch] = state_mms(h, ch, 512, sadds, [ev_b[ch], ev_yg, ev_wbo])
                nb0 = p.copy("dve", SB0, S0, waits=[sadds[ch][0], ev_yg], sig=True)
                nb1 = p.copy("dve", SB1, S1, waits=[sadds[ch][1]], sig=True)
                nb2 = p.copy("dve", SB2, S2, waits=[sadds[ch][2]], sig=True)
                ev_sbf = [nb0, nb1, nb2]
        ev_free = [ev_b[-1], ev_y[-1]] + \
            ([sadds[14][-1]] if 14 in sadds else [])
        if L1A_STOP["v"] == "y1" and h == 0:
            return ev_y[-1]
    if L1A_STOP["v"] == "yall":
        return ev_y[-1]

    ev_cf = []
    for ch in range(16):
        for g8 in range(8):
            src = c.XN[:, ch * 1024 + g8 * 128: ch * 1024 + g8 * 128 + 128]
            pst = (c.PS_T if (ch * 8 + g8) % 2 == 0 else c.PS_U)[:, 384:512]
            evt = p.transpose(pst, src, c.id128,
                              waits=[ev_y[-1]] + ([ev_cf[-2]] if len(ev_cf) >= 2 else []),
                              sig=True)
            dst = c.BIG0[:, g8 * 2048 + ch * 128: g8 * 2048 + ch * 128 + 128]
            e = p.copy("dve", dst, pst, waits=[evt, ev_am_hist[-1]], sig=True)
            ev_cf.append(e)
    wp = w["WoT"].rearrange("(cd pp) o -> pp cd o", pp=128)
    TT = 512
    ev_res = None
    ev_ogrp = []
    for oc in range(NC):
        wbuf = c.WA[:, 2048 + (oc % 2) * 1024: 2048 + (oc % 2) * 1024 + 1024]
        wwait = [ev_cf[-1], ev_am_hist[-1]] if oc < 2 else [ev_ogrp[(oc - 2) * 4 + 3]]
        ev_w = p.dma("gp", wbuf.rearrange("p (cd f) -> p cd f", cd=NC),
                     wp[:, :, oc * 128:(oc + 1) * 128], waits=wwait)
        for tt in range(4):
            ops = (c.PS_B if tt % 2 == 0 else c.PS_C)[:, 0:512]
            ev_mm = None
            for cd in range(NC):
                wv = []
                if cd == 0:
                    wv = [ev_w, ev_cf[-1]]
                    if ev_res is not None:
                        wv.append(ev_res)
                ev_mm = p.mm(ops, wbuf[:, cd * 128:(cd + 1) * 128],
                             c.BIG0[:, cd * T + tt * TT: cd * T + tt * TT + TT],
                             start=(cd == 0), stop=(cd == NC - 1), waits=wv,
                             sig=(cd == NC - 1))
            ev_ogrp.append(ev_mm)
            xf = c.XF[:, oc * T + tt * TT: oc * T + tt * TT + TT]
            ev_res = p.tt("dve", xf, xf, ops, ALU.add, waits=[ev_mm], sig=True)
    return ev_res


L1ATTN_WDECL = [
    ("WqT", (D, 256), F32), ("WkT", (D, 256), F32),
    ("WvT", (D, D), F32), ("WoT", (D, D), F32),
]


def build_l1attn_program():
    return _wrap_program(emit_l1attn, L1ATTN_WDECL)


def l1attn_inputs(ins, core):
    return {
        "WqT": np.ascontiguousarray(ins["L1_attn_Wq"].T),
        "WkT": np.ascontiguousarray(ins["L1_attn_Wk"].T),
        "WvT": np.ascontiguousarray(ins["L1_attn_Wv"].T),
        "WoT": np.ascontiguousarray(ins["L1_attn_Wo"].T),
        "meta": host_meta(core),
    }



def emit_l2attn(c, p, w, dep_evs):
    nc = c.nc
    ev_ln = emit_ln_full(c, p, dep_evs)
    gin = nc.dram_tensor("l2a_gin", [256, 1], F32, kind="Internal").ap()
    gout = nc.dram_tensor("l2a_gout", [512, 1], F32, kind="Internal").ap()
    kin = nc.dram_tensor("l2a_kin", [256, 1], F32, kind="Internal").ap()
    kout = nc.dram_tensor("l2a_kout", [512, 1], F32, kind="Internal").ap()
    sin = nc.dram_tensor("l2a_sin", [256, 64], F32, kind="Internal").ap()
    sout = nc.dram_tensor("l2a_sout", [512, 64], F32, kind="Internal").ap()
    ev_s8 = p.memset("gp", c.CONSTF[:, 4:5], 0.125, sig=True)

    ev_cw = []
    ct_sb = c.WB[0:64, 0:256]
    e_ct = p.dma("gp", ct_sb.rearrange("p (h cc) -> p h cc", h=16),
                 w["codeT"].rearrange("h d cc -> d h cc"))
    for which in range(2):
        for hg in range(2):
            for cd in range(NC):
                ps = (c.PS_B if len(ev_cw) % 2 == 0 else c.PS_C)[:, 0:128]
                mms = []
                for hh in range(8):
                    h = hg * 8 + hh
                    wsl = c.WB[0:64, 256 + (hh % 2) * 128: 256 + (hh % 2) * 128 + 128]
                    wwv = [mms[hh - 2]] if hh >= 2 else \
                        ([ev_cw[-1]] if ev_cw else [])
                    ev_w = p.dma("gp", wsl,
                                 w["wqkv_raw"][which * 1024 + h * 64: which * 1024 + (h + 1) * 64,
                                               cd * 128:(cd + 1) * 128], waits=wwv)
                    mmw = [ev_w, e_ct]
                    if hh == 0:
                        mmw += [ev_ln] if not ev_cw else [ev_cw[-2]] if len(ev_cw) >= 2 else []
                    ev_mm = p.mm(ps[:, hh * 16:(hh + 1) * 16], wsl,
                                 ct_sb[:, h * 16:(h + 1) * 16],
                                 start=True, stop=True, waits=mmw, sig=True)
                    mms.append(ev_mm)
                if which == 0:
                    tgt = c.SQ[:, (cd * 2 + hg) * 128:(cd * 2 + hg) * 128 + 128]
                else:
                    tgt = c.WA[:, 2048 + (cd * 2 + hg) * 128: 2048 + (cd * 2 + hg) * 128 + 128]
                e = p.act(tgt, ps, AF.Identity, bias=0.0, scale=c.CONSTF[:, 4:5],
                          waits=[mms[-1], ev_s8, ev_ln], sig=True)
                ev_cw.append(e)
    ev_cwd = ev_cw[-1]

    ev_qe = []
    ev_kcf = []
    for hg in range(2):
        for tt in range(16):
            for which in range(2):
                ps = (c.PS_B if which == 0 else c.PS_C)[:, 0:128]
                cw = c.SQ if which == 0 else c.WA
                off = 0 if which == 0 else 2048
                ev_mm = None
                for cd in range(NC):
                    wv = []
                    if cd == 0:
                        wv = [ev_cwd, ev_ln]
                        if which == 0 and ev_qe:
                            wv.append(ev_qe[-1])
                        if which == 1 and ev_kcf:
                            wv.append(ev_kcf[-1])
                    ev_mm = p.mm(ps, _xn_tile(c, cd, tt * 128, 128),
                                 cw[:, off + (cd * 2 + hg) * 128: off + (cd * 2 + hg) * 128 + 128],
                                 start=(cd == 0), stop=(cd == NC - 1), waits=wv,
                                 sig=(cd == NC - 1))
                if which == 0:
                    m8 = c.SMALL[:, 192:200]
                    ev_r = p.red("dve", m8, ps.rearrange("p (h cc) -> p h cc", h=8),
                                 ALU.max, negate=True, waits=[ev_mm], sig=True)
                    tmp = c.SCR32[:, 0:128]
                    m8b = m8.rearrange("p (h o) -> p h o", o=1).broadcast_to([128, 8, 16])
                    ev_s = p.tt("dve", tmp.rearrange("p (h cc) -> p h cc", h=8),
                                ps.rearrange("p (h cc) -> p h cc", h=8), m8b, ALU.add,
                                waits=[ev_r], sig=True)
                    qe = c.BIG1[:, hg * 2048 + tt * 128: hg * 2048 + tt * 128 + 128]
                    e = p.act(qe, tmp, AF.Exp, waits=[ev_s], sig=True)
                    ev_qe.append(e)
                else:
                    st = c.WB[0:128, 512:640]
                    e1 = p.act(st, ps, AF.Identity, bias=0.0,
                               waits=[ev_mm] + ([ev_kcf[-1]] if ev_kcf else []), sig=True)
                    pst = (c.PS_T if tt % 2 == 0 else c.PS_U)[:, 384:512]
                    evt = p.transpose(pst, st, c.id128,
                                      waits=[e1] + ([ev_kcf[-2]] if len(ev_kcf) >= 2 else []),
                                      sig=True)
                    dst = c.BIG1[:, 4096 + hg * 2048 + tt * 128:
                                 4096 + hg * 2048 + tt * 128 + 128]
                    e2 = p.copy("dve", dst, pst, waits=[evt], sig=True)
                    ev_kcf.append(e2)
    sums_all = c.SCR32[:, 1024:1280]
    qe_all = c.BIG1[:, 0:4096].rearrange("p (g cc) -> p g cc", cc=16)
    ev_sm = p.red("dve", sums_all, qe_all, ALU.add, waits=[ev_qe[-1]], sig=True)
    ev_sm = emit_recip(c, p, sums_all, sums_all, waits=[ev_sm])
    sb4 = sums_all.rearrange("p (g o) -> p g o", o=1).broadcast_to([128, 256, 16])
    ev_qsm = p.tt("dve", qe_all, qe_all, sb4, ALU.mult, waits=[ev_sm], sig=True)

    kmax = c.SMALL[:, 200:202]
    ev1 = p.red("dve", kmax[:, 0:1], c.BIG1[:, 4096:6144], ALU.max,
                waits=[ev_kcf[-1]], sig=True)
    ev2 = p.red("dve", kmax[:, 1:2], c.BIG1[:, 6144:8192], ALU.max,
                waits=[ev_kcf[-1]], sig=True)
    evd = p.dma("sp", gin.rearrange("(hg pp) o -> pp hg o", pp=128),
                kmax.rearrange("p (hg o) -> p hg o", o=1), waits=[ev1, ev2],
                allow_slow_non_contiguous=True)
    ev_ag1 = p.cc_allgather(gin, gout, waits=[evd])
    evda = p.dma("sp", kmax.rearrange("p (hg o) -> p hg o", o=1),
                 gout[0:256].rearrange("(hg pp) o -> pp hg o", pp=128), waits=[ev_ag1],
                 allow_slow_non_contiguous=True)
    evdb = p.dma("sp", c.SMALL[:, 202:204].rearrange("p (hg o) -> p hg o", o=1),
                 gout[256:512].rearrange("(hg pp) o -> pp hg o", pp=128), waits=[ev_ag1],
                 allow_slow_non_contiguous=True)
    ev_gm = p.tt("dve", kmax, kmax, c.SMALL[:, 202:204], ALU.max,
                 waits=[evda, evdb], sig=True)

    ev_ke = []
    ev_den = []
    for hg in range(2):
        kc = c.BIG1[:, 4096 + hg * 2048: 4096 + (hg + 1) * 2048]
        nmx = c.SMALL[:, 204 + hg: 205 + hg]
        e0 = p.ts("dve", nmx, kmax[:, hg:hg + 1], -1.0, None, ALU.mult,
                  waits=[ev_gm], sig=True)
        e = p.act(kc, kc, AF.Exp, bias=nmx[:, 0:1], waits=[e0], sig=True)
        den = c.SCR32[:, hg * 2048:(hg + 1) * 2048]
        e2 = p.op("dve", lambda e_, o=den, i=kc, d=c.ones_f.broadcast_to([128, 2048]):
                  e_.tensor_tensor_scan(o, i, d, 0.0, ALU.add, ALU.bypass),
                  waits=[e, c.ev_consts], sig=True)
        ev_ke.append(e)
        ev_den.append(e2)

    evk0 = p.dma("sp", kin[0:128], c.SCR32[:, 2047:2048], waits=[ev_den[0]])
    evk1 = p.dma("sp", kin[128:256], c.SCR32[:, 4095:4096], waits=[ev_den[1]])
    ev_ag1b = p.cc_allgather(kin, kout, waits=[evk0, evk1])
    ev_sr = p.dma("sp", c.SMALL[:, 206:208].rearrange("p (hg o) -> p hg o", o=1),
                  kout[0:256].rearrange("(hg pp) o -> pp hg o", pp=128),
                  waits=[ev_ag1b], allow_slow_non_contiguous=True)
    ev_qn = []
    for hg in range(2):
        for tt in range(16):
            src = c.BIG1[:, hg * 2048 + tt * 128: hg * 2048 + tt * 128 + 128]
            pst = (c.PS_T if tt % 2 == 0 else c.PS_U)[:, 384:512]
            evt = p.transpose(pst, src, c.id128,
                              waits=[ev_qsm, ev_kcf[-1]] +
                              ([ev_qn[-2]] if len(ev_qn) >= 2 else []), sig=True)
            dst = c.BIG1[:, 8704 + hg * 2048 + tt * 128:
                         8704 + hg * 2048 + tt * 128 + 128]
            e = p.copy("dve", dst, pst, waits=[evt], sig=True)
            ev_qn.append(e)
    ev_qnd = None
    for hg in range(2):
        den = c.SCR32[:, hg * 2048:(hg + 1) * 2048]
        ks = c.SMALL[:, 208 + hg:209 + hg]
        e0 = p.tt("dve", ks, c.SMALL[:, 206 + hg:207 + hg], c.META[:, 1:2],
                  ALU.mult, waits=[ev_sr, c.ev_meta], sig=True)
        e1 = p.ts("dve", den, den, ks[:, 0:1], 1e-9, ALU.add, ALU.add,
                  waits=[e0, ev_qn[-1]], sig=True)
        e4 = emit_recip(c, p, den, den, waits=[e1])
        e5 = p.ts("dve", den, den, 0.125, None, ALU.mult, waits=[e4], sig=True)
        qnf = c.BIG1[:, 8704 + hg * 2048: 8704 + (hg + 1) * 2048]
        ev_qnd = p.tt("dve", qnf, qnf, den, ALU.mult,
                      waits=[e5, ev_qn[hg * 16 + 15]], sig=True)

    ev_v = []
    for hp in range(8):
        wsl = c.WA[:, (hp % 2) * 1024:(hp % 2) * 1024 + 1024]
        ev_w = p.dma("gp", wsl.rearrange("p (cd f) -> p cd f", cd=NC),
                     w["wqkvT"].rearrange("(cd pp) o -> pp cd o", pp=128)
                     [:, :, 2048 + hp * 128: 2048 + (hp + 1) * 128],
                     waits=[ev_v[-16]] if len(ev_v) >= 16 else [ev_cwd])
        for tt in range(16):
            ps = (c.PS_B if tt % 2 == 0 else c.PS_C)[:, 0:128]
            ev_mm = None
            for cd in range(NC):
                wv = []
                if cd == 0:
                    wv = [ev_w, ev_qnd]
                    if len(ev_v) >= 2:
                        wv.append(ev_v[-2])
                ev_mm = p.mm(ps, _xn_tile(c, cd, tt * 128, 128),
                             wsl[:, cd * 128:(cd + 1) * 128],
                             start=(cd == 0), stop=(cd == NC - 1), waits=wv,
                             sig=(cd == NC - 1))
            dst = c.BIG0[:, tt * 1024 + hp * 128: tt * 1024 + hp * 128 + 128]
            e = p.copy("dve", dst, ps, waits=[ev_mm], sig=True)
            ev_v.append(e)

    ev_keh = {}
    ev_sadd = {}
    ev_sdma = {}
    st_sb = c.SCR32[0:16, 2304:3328]
    ev_stg = []
    for h in range(16):
        hg, hl = h // 8, h % 8
        keh = c.WB[0:16, (h % 2) * 2048:(h % 2) * 2048 + 2048]
        wv = [ev_ke[hg], ev_qnd]
        if h >= 2:
            wv.append(ev_sadd[(h - 2, 15)])
        ev_k1 = p.dma("gp", keh,
                      c.BIG1[hl * 16:(hl + 1) * 16, 4096 + hg * 2048: 4096 + (hg + 1) * 2048],
                      waits=wv)
        ev_keh[h] = ev_k1
        ev_zero = p.memset("gp", st_sb[:, h * 64:(h + 1) * 64], 0.0,
                           waits=[ev_qnd], sig=True)
        for ch in range(16):
            pst = (c.PS_T if ch % 2 == 0 else c.PS_U)[:, 256:272]
            evt = p.transpose(pst, keh[:, ch * 128:(ch + 1) * 128], c.id128[0:16, 0:16],
                              waits=[ev_k1, ev_qnd] +
                              ([ev_stg[-2]] if len(ev_stg) >= 2 else []), sig=True)
            ktm = c.BIG1[:, 8192 + (ch % 2) * 16: 8192 + (ch % 2) * 16 + 16]
            ev_km = p.copy("dve", ktm, pst, waits=[evt], sig=True)
            ev_stg.append(ev_km)
            vt = c.BIG0[:, ch * 1024 + (h // 2) * 128 + (h % 2) * 64:
                        ch * 1024 + (h // 2) * 128 + (h % 2) * 64 + 64]
            sps = (c.PS_B if ch % 2 == 0 else c.PS_C)[0:16, 192:256]
            e_s1 = p.mm(sps, ktm, vt, start=True, stop=True,
                        waits=[ev_km, ev_v[-1]] +
                        ([ev_sadd[(h, ch - 2)]] if (h, ch - 2) in ev_sadd else []),
                        sig=True)
            prev = ev_sadd.get((h, ch - 1), ev_zero)
            e_s2 = p.tt("dve", st_sb[:, h * 64:(h + 1) * 64],
                        st_sb[:, h * 64:(h + 1) * 64], sps, ALU.add,
                        waits=[e_s1, prev], sig=True)
            ev_sadd[(h, ch)] = e_s2
        ev_sdma[h] = p.dma("sp", sin[h * 16:(h + 1) * 16, 0:64],
                           st_sb[:, h * 64:(h + 1) * 64], waits=[ev_sadd[(h, 15)]])
    ev_ag2 = p.cc_allgather(sin, sout, waits=list(ev_sdma.values()))
    str_ = c.SCR32[:, 3328:3456]
    ev_st0 = p.dma("sp", str_.rearrange("p (hg x) -> p hg x", hg=2),
                   sout[0:256, :].rearrange("(hg pp) x -> pp hg x", pp=128),
                   waits=[ev_ag2, ev_qnd])
    ev_st1 = p.ts("dve", str_, str_, c.META[:, 1:2], None, ALU.mult,
                  waits=[ev_st0, c.ev_meta], sig=True)

    M128 = c.WB[:, 4224:4352]
    DTm = c.SCR32[:, 3584:3712]
    e_m0 = p.op("gp", lambda g: g.iota(DTm, pattern=[[-1, 128]], base=0,
                                       channel_multiplier=1,
                                       allow_small_or_imprecise_dtypes=True),
                waits=[ev_qnd], sig=True)
    ev_M = p.ts("dve", M128, DTm, 0.0, None, ALU.is_le, waits=[e_m0], sig=True)

    ev_y = []
    ev_am_hist = []
    ev_sadd2 = {}
    ev_srh = {}
    ev_q1d = None
    for h in range(16):
        hg, hl = h // 8, h % 8
        keh = c.WB[0:16, (h % 2) * 2048:(h % 2) * 2048 + 2048]
        qnh = c.SQ[0:16, 0:2048]
        wv = [ev_ke[hg], ev_qnd, ev_sadd[(15, 15)]]
        if h >= 2:
            wv.append(ev_y[-1])
        ev_k1 = p.dma("gp", keh,
                      c.BIG1[hl * 16:(hl + 1) * 16, 4096 + hg * 2048: 4096 + (hg + 1) * 2048],
                      waits=wv)
        ev_q1 = p.dma("gp", qnh,
                      c.BIG1[hl * 16:(hl + 1) * 16, 8704 + hg * 2048: 8704 + (hg + 1) * 2048],
                      waits=[ev_qnd] + ([ev_y[-1]] if h >= 1 else []))
        srh = c.SCR32[0:16, 3456 + (h % 2) * 64: 3456 + (h % 2) * 64 + 64]
        ev_s0 = p.dma("gp", srh, str_[hl * 16:(hl + 1) * 16, hg * 64: hg * 64 + 64],
                      waits=[ev_st1] + ([ev_srh[h - 2]] if h - 2 in ev_srh else []))
        ev_init = p.copy("dve", st_sb[:, h * 64:(h + 1) * 64], srh,
                         waits=[ev_s0, ev_sdma[h]] +
                         ([ev_sadd2[(h - 1, 14)]] if (h - 1, 14) in ev_sadd2 else []),
                         sig=True)
        ev_srh[h] = ev_init
        st_bf = c.BIG1[0:16, 8224:8288]
        ev_bf = p.copy("dve", st_bf, st_sb[:, h * 64:(h + 1) * 64],
                       waits=[ev_init] + ([ev_y[-1]] if ev_y else []), sig=True)
        for ch in range(16):
            atp = c.PS_A[:, (ch % 2) * 1024:(ch % 2) * 1024 + 128]
            ev_at = p.mm(atp, keh[:, ch * 128:(ch + 1) * 128],
                         qnh[:, ch * 128:(ch + 1) * 128], start=True, stop=True,
                         waits=[ev_k1, ev_q1] +
                         ([ev_am_hist[-2]] if len(ev_am_hist) >= 2 else []),
                         sig=True)
            ats = c.BIG1[:, 8448 + (ch % 2) * 128: 8448 + (ch % 2) * 128 + 128]
            ev_am = p.tt("dve", ats, atp, M128, ALU.mult,
                         waits=[ev_at, ev_M] + ([ev_y[-2]] if len(ev_y) >= 2 else []),
                         sig=True)
            ev_am_hist.append(ev_am)
            pst = (c.PS_T if ch % 2 == 0 else c.PS_U)[:, 256:272]
            evt = p.transpose(pst, keh[:, ch * 128:(ch + 1) * 128], c.id128[0:16, 0:16],
                              waits=[ev_k1, ev_stg[-2]], sig=True)
            ktm = c.BIG1[:, 8192 + (ch % 2) * 16: 8192 + (ch % 2) * 16 + 16]
            ev_km = p.copy("dve", ktm, pst, waits=[evt], sig=True)
            ev_stg.append(ev_km)
            vt = c.BIG0[:, ch * 1024 + (h // 2) * 128 + (h % 2) * 64:
                        ch * 1024 + (h // 2) * 128 + (h % 2) * 64 + 64]
            yps = (c.PS_B if ch % 2 == 0 else c.PS_C)[:, 128:192]
            e_y1 = p.mm(yps, ats, vt, start=True, stop=False,
                        waits=[ev_am, ev_v[-1],
                               ev_sadd2.get((h, ch - 1), ev_init)] +
                        ([ev_y[-2]] if len(ev_y) >= 2 else []))
            e_y2 = p.mm(yps, qnh[:, ch * 128:(ch + 1) * 128], st_bf,
                        start=False, stop=True, waits=[ev_bf], sig=True)
            dst = c.XN[:, ch * 1024 + (h // 2) * 128 + (h % 2) * 64:
                       ch * 1024 + (h // 2) * 128 + (h % 2) * 64 + 64]
            e_ye = p.copy("dve", dst, yps, waits=[e_y2], sig=True)
            ev_y.append(e_ye)
            if ch < 15:
                sps = (c.PS_C if ch % 2 == 0 else c.PS_B)[0:16, 192:256]
                e_s1 = p.mm(sps, ktm, vt, start=True, stop=True,
                            waits=[ev_km] +
                            ([ev_sadd2[(h, ch - 2)]] if (h, ch - 2) in ev_sadd2 else []),
                            sig=True)
                e_s2 = p.tt("dve", st_sb[:, h * 64:(h + 1) * 64],
                            st_sb[:, h * 64:(h + 1) * 64], sps, ALU.add,
                            waits=[e_s1, e_y2], sig=True)
                ev_sadd2[(h, ch)] = e_s2
                ev_bf = p.copy("dve", st_bf, st_sb[:, h * 64:(h + 1) * 64],
                               waits=[e_s2], sig=True)

    ev_cf = []
    for ch in range(16):
        for g8 in range(8):
            src = c.XN[:, ch * 1024 + g8 * 128: ch * 1024 + g8 * 128 + 128]
            pst = (c.PS_T if (ch * 8 + g8) % 2 == 0 else c.PS_U)[:, 384:512]
            evt = p.transpose(pst, src, c.id128,
                              waits=[ev_y[-1]] + ([ev_cf[-2]] if len(ev_cf) >= 2 else []),
                              sig=True)
            dst = c.BIG0[:, g8 * 2048 + ch * 128: g8 * 2048 + ch * 128 + 128]
            e = p.copy("dve", dst, pst, waits=[evt], sig=True)
            ev_cf.append(e)
    wp = w["w_outT"].rearrange("(cd pp) o -> pp cd o", pp=128)
    TT = 512
    ev_res = None
    ev_ogrp = []
    for oc in range(NC):
        wbuf = c.WA[:, (oc % 2) * 1024:(oc % 2) * 1024 + 1024]
        wwait = [ev_cf[-1]] if oc < 2 else [ev_ogrp[(oc - 2) * 4 + 3]]
        ev_w = p.dma("gp", wbuf.rearrange("p (cd f) -> p cd f", cd=NC),
                     wp[:, :, oc * 128:(oc + 1) * 128], waits=wwait)
        for tt in range(4):
            ops = (c.PS_B if tt % 2 == 0 else c.PS_C)[:, 0:512]
            ev_mm = None
            for cd in range(NC):
                wv = []
                if cd == 0:
                    wv = [ev_w, ev_cf[-1]]
                    if ev_res is not None:
                        wv.append(ev_res)
                ev_mm = p.mm(ops, wbuf[:, cd * 128:(cd + 1) * 128],
                             c.BIG0[:, cd * T + tt * TT: cd * T + tt * TT + TT],
                             start=(cd == 0), stop=(cd == NC - 1), waits=wv,
                             sig=(cd == NC - 1))
            ev_ogrp.append(ev_mm)
            xf = c.XF[:, oc * T + tt * TT: oc * T + tt * TT + TT]
            ev_res = p.tt("dve", xf, xf, ops, ALU.add, waits=[ev_mm], sig=True)
    return ev_res


L2ATTN_WDECL = [
    ("wqkvT", (D, 3072), F32), ("wqkv_raw", (3072, D), F32),
    ("codeT", (16, 64, 16), F32), ("w_outT", (D, D), F32),
]


def build_l2attn_program():
    return _wrap_program(emit_l2attn, L2ATTN_WDECL)


def l2attn_inputs(ins, core):
    return {
        "wqkvT": np.ascontiguousarray(ins["L2_attn_w_qkv"].T),
        "wqkv_raw": np.ascontiguousarray(ins["L2_attn_w_qkv"]),
        "codeT": np.ascontiguousarray(ins["L2_attn_fc_code"][0].transpose(0, 2, 1)),
        "w_outT": np.ascontiguousarray(ins["L2_attn_w_out"].T),
        "meta": host_meta(core),
    }


def _xn_tile(c, cd, tok0, ntok):
    return c.XN[:, cd * T + tok0: cd * T + tok0 + ntok]



def _colform(vec, nchunks):
    return np.ascontiguousarray(vec.reshape(nchunks, 128).T.astype(np.float32))


def host_meta(core):
    m = np.zeros((128, 8), np.float32)
    m[:, 0] = (core % 2) * T
    m[:, 1] = float(core % 2)
    return m


def shard_x(xfull):
    out = []
    for cc in range(NCORES):
        b, h = cc // 2, cc % 2
        out.append(np.ascontiguousarray(xfull[b, h * T:(h + 1) * T, :].T))
    return out


def unshard_x(shards):
    out = np.empty((4, SEQ, D), np.float32)
    for cc in range(NCORES):
        b, h = cc // 2, cc % 2
        out[b, h * T:(h + 1) * T, :] = shards[cc].T
    return out



def _io_tensors(nc):
    x_in = nc.dram_tensor("x_in", [D, T], F32, kind="ExternalInput").ap()
    x_out = nc.dram_tensor("x_out", [D, T], F32, kind="ExternalOutput").ap()
    meta = nc.dram_tensor("meta", [128, 8], F32, kind="ExternalInput").ap()
    return x_in, x_out, meta


DBG = {"on": False}


def _wrap_program(emit_fn, wdecl):
    nc = bass.Bass("TRN2", target_bir_lowering=False, debug=False)
    x_in, x_out, meta = _io_tensors(nc)
    w = {}
    for (name, shape, dt) in wdecl:
        w[name] = nc.dram_tensor(name, list(shape), dt, kind="ExternalInput").ap()
    with ExitStack() as es:
        c = alloc_common(nc, es)
        p = Prog(nc)
        p.make_sems(es)
        block = es.enter_context(nc.Block())
        if DBG["on"]:
            c.dbg = nc.dram_tensor("dbg", [128, 512], F32, kind="ExternalOutput").ap()
            c.dbg2 = nc.dram_tensor("dbg2", [128, 16384], F32, kind="ExternalOutput").ap()
        else:
            c.dbg = None
            c.dbg2 = None
        emit_consts(c, p, meta)
        ev_x = p.dma("sp", c.XF[:, :].rearrange("p (cd s) -> p cd s", cd=NC),
                     x_in.rearrange("(cd p) s -> p cd s", p=128))
        ev = emit_fn(c, p, w, [ev_x])
        ev_o = p.dma("sp", x_out.rearrange("(cd p) s -> p cd s", p=128),
                     c.XF[:, :].rearrange("p (cd s) -> p cd s", cd=NC),
                     waits=[ev] + p.all_events())
        p._emit("sp", None, [ev_o], None, 0)
        p.finalize_into(block)
    return nc


L0FFN_WDECL = [
    ("w1T", (D, 4096), F32), ("w2", (4096, D), F32),
    ("b1c", (128, 32), F32), ("b2c", (128, 8), F32),
    ("gamc", (128, 8), F32), ("betc", (128, 8), F32),
]


def build_l0ffn_program():
    return _wrap_program(emit_l0ffn, L0FFN_WDECL)


def l0ffn_inputs(ins, core):
    return {
        "w1T": np.ascontiguousarray(ins["L0_ffn_w1"][0]),
        "w2": np.ascontiguousarray(ins["L0_ffn_w2"][0]),
        "b1c": _colform(ins["L0_ffn_b1"][0], 32),
        "b2c": _colform(ins["L0_ffn_b2"][0], 8),
        "gamc": _colform(ins["L0_ffn_gamma"], 8),
        "betc": _colform(ins["L0_ffn_beta"], 8),
        "meta": host_meta(core),
    }


L1FFN_WDECL = [
    ("qwT", (D, D), F32), ("kwT", (D, D), F32),
    ("vw", (64, D, 128), F32), ("ow", (64, 128, D), F32),
    ("svT", (D, 64), F32), ("soT", (D, 64), F32), ("rs_rep", (128, 1), F32),
]


def build_l1ffn_program():
    return _wrap_program(lambda c, p, w, d: emit_l1ffn(c, p, w, d, "l1f"),
                         L1FFN_WDECL)


def l1ffn_inputs(ins, core):
    return {
        "qwT": np.ascontiguousarray(ins["L1_ffn_q_w"].T),
        "kwT": np.ascontiguousarray(ins["L1_ffn_k_w"].T),
        "vw": np.ascontiguousarray(ins["L1_ffn_v_w"]),
        "ow": np.ascontiguousarray(ins["L1_ffn_o_w"]),
        "svT": np.ascontiguousarray(ins["L1_ffn_sel_v_w"].T),
        "soT": np.ascontiguousarray(ins["L1_ffn_sel_o_w"].T),
        "rs_rep": np.broadcast_to(ins["L1_ffn_route_scale"][:, None], (128, 1)).copy(),
        "meta": host_meta(core),
    }


L0ATTN_WDECL = [
    ("w_qkv", (D, 3072), F32), ("w_out", (D, D), F32),
    ("dirs", (3, 8), F32), ("hs_rep", (128, 8), F32),
]


def build_l0attn_program():
    return _wrap_program(emit_l0attn, L0ATTN_WDECL)


def l0attn_inputs(ins, core):
    t = np.array([[(i >> 2 & 1) * 2 - 1, (i >> 1 & 1) * 2 - 1, (i & 1) * 2 - 1]
                  for i in range(8)], dtype=np.float32)
    dirs = (t / np.sqrt(3.0, dtype=np.float32)).T.copy()
    return {
        "w_qkv": np.ascontiguousarray(ins["L0_attn_w_qkv"]),
        "w_out": np.ascontiguousarray(ins["L0_attn_w_out"]),
        "dirs": dirs,
        "hs_rep": np.broadcast_to(ins["L0_attn_head_scales"][None, :], (128, 8)).copy(),
        "meta": host_meta(core),
    }


L3ATTN_WDECL = [
    ("Wqkv", (D, 3072), F32), ("bqkvc", (128, 24), F32),
    ("Wproj", (D, D), F32), ("bprojc", (128, 8), F32),
]


def build_l3attn_program():
    return _wrap_program(emit_l3attn, L3ATTN_WDECL)


def l3attn_inputs(ins, core):
    return {
        "Wqkv": np.ascontiguousarray(ins["L3_attn_Wqkv"]),
        "bqkvc": _colform(ins["L3_attn_bqkv"], 24),
        "Wproj": np.ascontiguousarray(ins["L3_attn_Wproj"]),
        "bprojc": _colform(ins["L3_attn_bproj"], 8),
        "meta": host_meta(core),
    }


L3FFN_WDECL = [
    ("router_w", (D, 8), F32), ("rb_rep", (128, 8), F32),
    ("upT", (8, D, 4096), F32), ("downT", (8, 4096, D), F32),
]


def build_l3ffn_program():
    return _wrap_program(emit_l3ffn, L3FFN_WDECL)


def l3ffn_inputs(ins, core):
    return {
        "router_w": np.ascontiguousarray(ins["L3_ffn_router_w"]),
        "rb_rep": np.broadcast_to(ins["L3_ffn_router_b"][None, :], (128, 8)).copy(),
        "upT": np.ascontiguousarray(ins["L3_ffn_expert_up"].transpose(0, 2, 1)),
        "downT": np.ascontiguousarray(ins["L3_ffn_expert_down"].transpose(0, 2, 1)),
        "meta": host_meta(core),
    }


L2FFN_WDECL = [
    ("wgT", (D, 3072), I32), ("wuT", (D, 3072), I32), ("wdT", (3072, D), I32),
]


def build_l2ffn_program():
    return _wrap_program(emit_l2ffn, L2FFN_WDECL)


def l2ffn_inputs(ins, core):
    return {
        "wgT": np.ascontiguousarray(ins["L2_ffn_w_gate"].T),
        "wuT": np.ascontiguousarray(ins["L2_ffn_w_up"].T),
        "wdT": np.ascontiguousarray(ins["L2_ffn_w_down"].T),
        "meta": host_meta(core),
    }


MK_STEPS = [
    (build_l0ffn_program, l0ffn_inputs),
    (build_l0attn_program, l0attn_inputs),
    (build_l0ffn_program, l0ffn_inputs),
    (build_l1ffn_program, l1ffn_inputs),
    (build_l1attn_program, l1attn_inputs),
    (build_l1ffn_program, l1ffn_inputs),
    (build_l2ffn_program, l2ffn_inputs),
    (build_l2attn_program, l2attn_inputs),
    (build_l2ffn_program, l2ffn_inputs),
    (build_l3ffn_program, l3ffn_inputs),
    (build_l3attn_program, l3attn_inputs),
    (build_l3ffn_program, l3ffn_inputs),
]


def kernel(**inputs):
    shards = shard_x(np.asarray(inputs["L0_attn_x"]))
    built = {}
    for build, mkin in MK_STEPS:
        if build not in built:
            built[build] = build()
        nc = built[build]
        in_maps = []
        for cc in range(NCORES):
            im = mkin(inputs, cc)
            im["x_in"] = shards[cc]
            in_maps.append(im)
        res = run_bass_kernel_spmd(nc, in_maps, core_ids=list(range(NCORES)))
        shards = [np.asarray(res.results[cc]["x_out"]) for cc in range(NCORES)]
    return unshard_x(shards)
